# version 10
# baseline (speedup 1.0000x reference)
import numpy as np
import ml_dtypes
from contextlib import ExitStack

import concourse.bass as bass
import concourse.mybir as mybir
from concourse.bass_utils import run_bass_kernel_spmd

F32 = mybir.dt.float32
BF16 = mybir.dt.bfloat16
I32 = mybir.dt.int32
AF = mybir.ActivationFunctionType
ALU = mybir.AluOpType
AX = mybir.AxisListType

D = 1024
H = 8
HD = 128
HID = 3072
KC = D // 128
HC = HID // 128
S127 = 1.0 / 127.0
EPS = 1e-5
N_CORES = 8

SYNC_SEMS = ("LXM", "S1A", "S1B", "S1C", "S2A", "S2B", "S2C", "S2D",
             "LXOA", "LXOB")
GP_SEMS = ("LWA", "LWR", "LWG", "LWD", "LWF", "LWF2", "LWB", "LWC", "LWGB")
ALL_SEMS = SYNC_SEMS + GP_SEMS + ("CC", "SA", "SV", "SP", "SL")


class Sched:

    def __init__(self):
        self.plan = []
        self.cnt = {k: 0 for k in ALL_SEMS}

    def op(self, engine, fn, waits=None, inc=None):
        waits = {k: v for k, v in (waits or {}).items() if v and v > 0}
        incs_abs = []
        for sem, amount in (inc or []):
            self.cnt[sem] += amount
            incs_abs.append((sem, amount, self.cnt[sem]))
        self.plan.append((engine, fn, waits, incs_abs))
        return {s: a for s, _, a in incs_abs}

    def dma(self, engine, fn, sem, waits=None):
        return self.op(engine, fn, waits, inc=[(sem, 16)])[sem]

    def act(self, fn, waits=None):
        w = dict(waits or {})
        w["SA"] = max(w.get("SA", 0), self.cnt["SA"])
        return self.op("scalar", fn, w, inc=[("SA", 1)])["SA"]

    def dve(self, fn, waits=None):
        w = dict(waits or {})
        w["SV"] = max(w.get("SV", 0), self.cnt["SV"])
        return self.op("vector", fn, w, inc=[("SV", 1)])["SV"]

    def pool(self, fn, waits=None):
        w = dict(waits or {})
        w["SL"] = max(w.get("SL", 0), self.cnt["SL"])
        return self.op("gpsimd", fn, w, inc=[("SL", 1)])["SL"]

    def pe(self, fn, waits=None, group_end=True):
        if group_end:
            return self.op("tensor", fn, waits, inc=[("SP", 1)])["SP"]
        self.op("tensor", fn, waits)
        return None


def emit_plan(sc, block, sems):
    def emit(engine_name, eng):
        high = {}
        for (en, fn, waits, incs) in sc.plan:
            if en != engine_name:
                continue
            for s, v in sorted(waits.items()):
                if high.get(s, 0) >= v:
                    continue
                eng.wait_ge(sems[s], v)
                high[s] = v
            inst = fn(eng)
            for s, amount, absval in incs:
                inst = inst.then_inc(sems[s], amount)

    @block.sync
    def _(sync):
        emit("sync", sync)
        for s in SYNC_SEMS:
            if sc.cnt[s] > 0:
                sync.wait_ge(sems[s], sc.cnt[s])

    @block.gpsimd
    def _(g):
        emit("gpsimd", g)
        for s in GP_SEMS:
            if sc.cnt[s] > 0:
                g.wait_ge(sems[s], sc.cnt[s])

    @block.scalar
    def _(s):
        emit("scalar", s)

    @block.vector
    def _(v):
        emit("vector", v)

    @block.tensor
    def _(t):
        emit("tensor", t)


def build_full(T):
    NT1 = T // 128
    NT2 = T // 256
    nc = bass.Bass("TRN2", target_bir_lowering=False, debug=False)

    x_h = nc.dram_tensor("x", [T, D], F32, kind="ExternalInput").ap()
    wrT_h = nc.dram_tensor("wrT", [D, D], F32, kind="ExternalInput").ap()
    wfT_h = nc.dram_tensor("wfT", [D, D], F32, kind="ExternalInput").ap()
    wgT_h = nc.dram_tensor("wgT", [D, HID], I32, kind="ExternalInput").ap()
    wuT_h = nc.dram_tensor("wuT", [D, HID], I32, kind="ExternalInput").ap()
    wdT_h = nc.dram_tensor("wdT", [HID, D], I32, kind="ExternalInput").ap()
    attr_row_h = nc.dram_tensor("attr_row", [1, HD], F32, kind="ExternalInput").ap()
    attr_col_h = nc.dram_tensor("attr_col", [HD, 1], F32, kind="ExternalInput").ap()
    gam_h = nc.dram_tensor("gam", [1, D], F32, kind="ExternalInput").ap()
    bet_h = nc.dram_tensor("bet", [1, D], F32, kind="ExternalInput").ap()
    idn_h = nc.dram_tensor("idn", [128, 128], BF16, kind="ExternalInput").ap()
    ones_bf_h = nc.dram_tensor("ones_bf", [128, 1], BF16, kind="ExternalInput").ap()
    ones_row_h = nc.dram_tensor("ones_row", [1, 128], F32, kind="ExternalInput").ap()
    ones_rowb_h = nc.dram_tensor("ones_rowb", [1, 128], BF16, kind="ExternalInput").ap()
    out_h = nc.dram_tensor("out", [T, D], F32, kind="ExternalOutput").ap()
    bounce_in = nc.dram_tensor("bounce_in", [128, 16], F32)
    bounce_out = nc.dram_tensor("bounce_out", [128, 16], F32)

    es = ExitStack()
    SBp = lambda n, s, d: es.enter_context(nc.sbuf_tensor(n, s, d))

    wgT = SBp("wgT_sb", [128, KC * HID], BF16)
    wuT = SBp("wuT_sb", [128, KC * HID], BF16)
    wdT = SBp("wdT_sb", [128, HC * D], BF16)
    idn = SBp("idn_sb", [128, 128], BF16)
    ones_bf = SBp("ones_bf_sb", [128, 1], BF16)
    ones_row = SBp("ones_row_sb", [1, 128], F32)
    ones_rowb = SBp("ones_rowb_sb", [1, 128], BF16)
    attr_col = SBp("attr_col_sb", [HD, 1], F32)
    recip_a = SBp("recip_a", [HD, 1], F32)
    rstd_all = SBp("rstd_all", [128, NT1], F32)
    negmur_all = SBp("negmur_all", [128, NT1], F32)
    ct_bf = SBp("ct_bf", [128, H], BF16)
    eps_col = SBp("eps_col", [128, 1], F32)
    gam_b = SBp("gam_b", [128, D], BF16)
    bet_b = SBp("bet_b", [128, D], BF16)

    tr_ps = es.enter_context(nc.psum_tensor("tr_ps", [128, 1024], BF16))
    pes = ExitStack()
    fr_ps = [pes.enter_context(nc.psum_tensor(f"fr_ps{i}", [128, 1024], F32))
             for i in range(2)]
    ct_ps = pes.enter_context(nc.psum_tensor("ct_ps", [128, 512], F32))
    bc_ps = pes.enter_context(nc.psum_tensor("bc_ps", [128, 1024], F32))

    s1bes = ExitStack()
    S1B = lambda n, s, d: s1bes.enter_context(nc.sbuf_tensor(n, s, d))
    wfc = [S1B(f"wfc_{i}", [128, D], BF16) for i in range(2)]
    ctall = S1B("ctall", [128, 8], F32)
    sall = S1B("sall", [1, 8], F32)
    sall_r = S1B("sall_r", [1, 8], F32)
    rs_b = S1B("rs_b", [128, 8], F32)
    ct1 = S1B("ct1", [128, 8], F32)
    ct2 = S1B("ct2", [128, 8], F32)

    s1es = ExitStack()
    S1 = lambda n, s, d: s1es.enter_context(nc.sbuf_tensor(n, s, d))
    wrT_s = S1("wrT_s", [128, KC * D], BF16)
    attr_rep = S1("attr_rep", [128, D], BF16)
    x1 = [S1(f"x1_{i}", [128, D], F32) for i in range(3)]
    xn1 = [S1(f"xn1_{i}", [128, D], BF16) for i in range(2)]
    xnT1 = [S1(f"xnT1_{i}", [128, D], BF16) for i in range(2)]
    frbf = [S1(f"frbf_{i}", [128, D], BF16) for i in range(2)]
    bnt1 = [S1(f"bnt1_{i}", [128, 12], F32) for i in range(2)]
    mv1 = [S1(f"mv1_{i}", [128, 2], F32) for i in range(2)]
    lnv1 = [S1(f"lnv1_{i}", [128, 1], F32) for i in range(2)]
    srab = [S1(f"srab_{i}", [128, 16], F32) for i in range(2)]
    srs = [S1(f"srs_{i}", [128, 8], F32) for i in range(2)]
    e_bf = [S1(f"e_bf_{i}", [128, 8], BF16) for i in range(2)]
    cs_stage = S1("cs_stage", [128, 16], F32)

    sems_es = ExitStack()
    sems = {k: sems_es.enter_context(nc.semaphore(k)) for k in ALL_SEMS}
    sc = Sched()

    sc.dma("sync", lambda e: e.dma_start(out=idn[:, :], in_=idn_h), "LXM")
    sc.dma("sync", lambda e: e.dma_start(out=ones_bf[:, :], in_=ones_bf_h), "LXM")
    sc.dma("sync", lambda e: e.dma_start(out=ones_row[:, :], in_=ones_row_h), "LXM")
    sc.dma("sync", lambda e: e.dma_start(out=ones_rowb[:, :], in_=ones_rowb_h), "LXM")
    sc.dma("sync", lambda e: e.dma_start(out=attr_col[:, :], in_=attr_col_h), "LXM")
    LXM_T = sc.cnt["LXM"]

    for c in range(KC):
        sc.dma("gpsimd", lambda e, c=c: e.dma_start(
            out=attr_rep[0:1, c * HD:(c + 1) * HD], in_=attr_row_h), "LWA")
    LWA_T = sc.cnt["LWA"]
    sc.dma("gpsimd", lambda e: e.dma_start(out=gam_b[0:1, :], in_=gam_h), "LWGB")
    sc.dma("gpsimd", lambda e: e.dma_start(out=bet_b[0:1, :], in_=bet_h), "LWGB")
    sc.dma("gpsimd", lambda e: e.dma_start(
        out=wrT_s[:, :].rearrange("p (kc n) -> p kc n", kc=KC),
        in_=wrT_h.rearrange("(kc p) n -> p kc n", p=128)), "LWR")
    sc.dma("gpsimd", lambda e: e.dma_start(
        out=wgT[:, :].rearrange("p (kc n) -> p kc n", kc=KC),
        in_=wgT_h.rearrange("(kc p) n -> p kc n", p=128)), "LWG")
    sc.dma("gpsimd", lambda e: e.dma_start(
        out=wuT[:, :].rearrange("p (kc n) -> p kc n", kc=KC),
        in_=wuT_h.rearrange("(kc p) n -> p kc n", p=128)), "LWG")
    sc.dma("gpsimd", lambda e: e.dma_start(
        out=wdT[:, :].rearrange("p (hc n) -> p hc n", hc=HC),
        in_=wdT_h.rearrange("(hc p) n -> p hc n", p=128)), "LWD")

    sv_eps = sc.dve(lambda e: e.memset(eps_col[:, :], EPS))
    sc.dve(lambda e: e.memset(cs_stage[:, :], 0.0))
    sv_recipa = sc.dve(lambda e: e.reciprocal(recip_a[:, :], attr_col[:, :]),
                       waits={"LXM": LXM_T})

    def bcast_bf(row_tile, w0):
        sp = 0
        for half in range(2):
            sp = sc.pe(lambda e, half=half: e.matmul(
                bc_ps[:, half * 512:(half + 1) * 512], ones_rowb[0:1, :],
                row_tile[0:1, half * 512:(half + 1) * 512], start=True,
                stop=True, skip_group_check=True),
                waits=w0 if half == 0 else None, group_end=(half == 1))
        return sc.dve(lambda e: e.tensor_copy(row_tile[:, :], bc_ps[:, 0:1024]),
                      waits={"SP": sp})

    sv_attr = bcast_bf(attr_rep, {"LWA": LWA_T, "LXM": LXM_T})
    sv_gb = bcast_bf(gam_b, {"LWGB": 32, "SV": sv_attr})
    sv_bb = bcast_bf(bet_b, {"LWGB": 32, "SV": sv_gb})

    sv_wrs = 0
    for c in range(KC):
        sc.dve(lambda e, c=c: e.tensor_mul(frbf[0][:, :],
                                           wrT_s[:, c * D:(c + 1) * D],
                                           attr_rep[:, :]),
               waits={"SV": sv_attr, "LWR": 16})
        sv_wrs = sc.dve(lambda e, c=c: e.tensor_copy(wrT_s[:, c * D:(c + 1) * D],
                                                     frbf[0][:, :]))


    way_xn = [0] * NT1
    way_stats = [0] * NT1
    way_evac = [0] * NT1
    way_red = [0] * NT1
    way_frbf = [0] * NT1
    way_trg = [0] * NT1
    way_frg = [0] * NT1
    way_csg = [0] * NT1
    way_e = [0] * NT1
    way_srs = [0] * NT1
    lx_x1 = [0] * NT1
    S1SEM = ("S1A", "S1B", "S1C")

    def s1_load(t):
        w = {}
        if t >= 3:
            w = {"SA": way_xn[t - 3], "SV": way_stats[t - 3]}
        lx_x1[t] = sc.dma("sync", lambda e, t=t: e.dma_start(
            out=x1[t % 3][:, :], in_=x_h[t * 128:(t + 1) * 128, :]),
            S1SEM[t % 3], waits=w)

    def s1_stats(t):
        p = t % 2
        w = {S1SEM[t % 3]: lx_x1[t]}
        if t >= 2:
            w["SA"] = way_xn[t - 2]
        sc.dve(lambda e, t=t, p=p: e.bn_stats(bnt1[p][:, 0:6],
                                              x1[t % 3][:, 0:512]), waits=w)
        sc.dve(lambda e, t=t, p=p: e.bn_stats(bnt1[p][:, 6:12],
                                              x1[t % 3][:, 512:1024]))
        way_stats[t] = sc.dve(lambda e, p=p: e.bn_aggr(mv1[p][:, :],
                                                       bnt1[p][:, :]))

    def s1_lnxn(t):
        p = t % 2
        sa_ln = sc.act(lambda e, p=p: e.activation(
            lnv1[p][:, :], mv1[p][:, 1:2], AF.Ln, bias=eps_col[:, :]),
            waits={"SV": max(way_stats[t], sv_eps)})
        sa_rstd = sc.act(lambda e, t=t, p=p: e.activation(
            rstd_all[:, t:t + 1], lnv1[p][:, :], AF.Exp, scale=-0.5))
        sl_nmr = sc.pool(lambda e, t=t, p=p: e.tensor_scalar(
            negmur_all[:, t:t + 1], mv1[p][:, 0:1], rstd_all[:, t:t + 1], -1.0,
            ALU.mult, ALU.mult), waits={"SA": sa_rstd})
        w = {"SL": sl_nmr, S1SEM[t % 3]: lx_x1[t]}
        if t >= 2:
            w["SP"] = way_trg[t - 2]
        way_xn[t] = sc.act(lambda e, p=p, t=t: e.activation(
            xn1[p][:, :], x1[t % 3][:, :], AF.Identity,
            bias=negmur_all[:, t:t + 1], scale=rstd_all[:, t:t + 1]), waits=w)

    def s1_cs(t):
        p = t % 2
        w = {"SA": max(way_e[t], way_frbf[t]), "LXM": LXM_T}
        for h in range(H):
            sc.pe(lambda e, h=h, p=p, t=t: e.matmul(
                ct_ps[:, h:h + 1], frbf[p][:, h * HD:(h + 1) * HD],
                e_bf[p][:, h:h + 1], start=(t == 0 and h == 0),
                stop=(t == NT1 - 1), skip_group_check=True),
                waits=w if h == 0 else None, group_end=False)
        way_csg[t] = sc.pe(lambda e, p=p, t=t: e.matmul(
            ct_ps[0:1, 8:16], ones_bf[:, :], e_bf[p][:, :],
            start=False, stop=(t == NT1 - 1), skip_group_check=True))

    def s1_tr(t):
        p = t % 2
        w = {"SA": max(way_xn[t], way_evac[t - 1] if t >= 1 else 0),
             "LXM": LXM_T}
        for c in range(KC):
            way_trg[t] = sc.pe(lambda e, c=c, p=p: e.transpose(
                tr_ps[:, c * 128:(c + 1) * 128], xn1[p][:, c * 128:(c + 1) * 128],
                idn[:, :]), waits=w if c == 0 else None, group_end=(c == KC - 1))
        way_evac[t] = sc.act(lambda e, t=t: e.activation(
            xnT1[t % 2][:, :], tr_ps[:, :], AF.Copy), waits={"SP": way_trg[t]})

    def s1_fr(t):
        p = t % 2
        w = {"SA": way_evac[t], "SV": sv_wrs}
        if t >= 2:
            w["SA"] = max(w.get("SA", 0), way_frbf[t - 2])
        for half in range(2):
            for kc in range(KC):
                way_frg[t] = sc.pe(lambda e, half=half, kc=kc, p=p, t=t: e.matmul(
                    fr_ps[p][:, half * 512:(half + 1) * 512],
                    xnT1[t % 2][:, kc * 128:(kc + 1) * 128],
                    wrT_s[:, kc * D + half * 512: kc * D + (half + 1) * 512],
                    start=(kc == 0), stop=(kc == KC - 1), skip_group_check=True),
                    waits=w if (half == 0 and kc == 0) else None,
                    group_end=(half == 1 and kc == KC - 1))

    def s1_post(t):
        p = t % 2
        w = {"SP": way_frg[t]}
        if t >= 2:
            w["SP"] = max(w["SP"], way_csg[t - 2])
            w["SV"] = way_red[t - 2]
        way_frbf[t] = sc.act(lambda e, p=p: e.activation(
            frbf[p][:, :], fr_ps[p][:, :], AF.Copy), waits=w)
        sc.dve(lambda e, p=p: e.reduce_sum(
            srab[p][:, 0:8],
            frbf[p][:, :].rearrange("q (h j) -> q h j", h=H), axis=AX.X),
            waits={"SA": way_frbf[t]})
        way_red[t] = sc.dve(lambda e, p=p: e.reduce_sum(
            srab[p][:, 8:16],
            frbf[p][:, :].rearrange("q (h j) -> q h j", h=H), axis=AX.X,
            apply_absolute_value=True))
        way_srs[t] = sc.dve(lambda e, p=p: e.scalar_tensor_tensor(
            srs[p][:, :], srab[p][:, 0:8], 0.505 / 0.495, srab[p][:, 8:16],
            ALU.mult, ALU.add))
        w = {"SV": way_srs[t]}
        if t >= 2:
            w["SP"] = max(w.get("SP", 0), way_csg[t - 2])
        way_e[t] = sc.act(lambda e, p=p: e.activation(
            e_bf[p][:, :], srs[p][:, :], AF.Exp, scale=0.495), waits=w)

    for i in range(NT1 + 4):
        if i < NT1:
            s1_load(i)
        if 1 <= i <= NT1:
            s1_stats(i - 1)
            s1_lnxn(i - 1)
            s1_tr(i - 1)
        if 2 <= i <= NT1 + 1:
            s1_fr(i - 2)
        if 3 <= i <= NT1 + 2:
            s1_post(i - 3)
        if 4 <= i <= NT1 + 3:
            s1_cs(i - 4)

    sv_ct = sc.dve(lambda e: e.tensor_copy(cs_stage[:, 0:8], ct_ps[:, 0:8]),
                   waits={"SP": way_csg[NT1 - 1]})
    sv_s = sc.dve(lambda e: e.tensor_copy(cs_stage[0:1, 8:16], ct_ps[0:1, 8:16]))
    sc.dma("gpsimd", lambda e: e.dma_start(
        out=bounce_in.ap()[:, :], in_=cs_stage[:, :]), "LWB", waits={"SV": sv_s})
    sc.op("gpsimd", lambda e: e.collective_compute(
        "AllReduce", ALU.add, replica_groups=[[0, 1, 2, 3], [4, 5, 6, 7]],
        ins=[bounce_in.ap().opt()], outs=[bounce_out.ap().opt()]),
        waits={"LWB": 16}, inc=[("CC", 1)])
    sc.dma("gpsimd", lambda e: e.dma_start(
        out=ctall[:, :], in_=bounce_out.ap()[:, 0:8]), "LWC", waits={"CC": 1})
    sc.dma("gpsimd", lambda e: e.dma_start(
        out=sall[:, :], in_=bounce_out.ap()[0:1, 8:16]), "LWC")


    s1es.close()
    S2es = ExitStack()
    S2 = lambda n, s, d: S2es.enter_context(nc.sbuf_tensor(n, s, d))
    F_b = S2("F_b", [128, D], F32)
    x2 = [S2(f"x2_{i}", [128, D], F32) for i in range(4)]
    xn2 = S2("xn2", [128, D], BF16)
    xnT2 = S2("xnT2", [128, KC * 256], BF16)
    hdn = [S2(f"hdn_{i}", [128, 256], BF16) for i in range(3)]
    silu_sb = [S2(f"silu_{i}", [128, 256], BF16) for i in range(3)]
    A2 = [S2(f"A2_{i}", [128, D], F32) for i in range(2)]
    B2 = [S2(f"B2_{i}", [128, D], F32) for i in range(2)]
    C2 = [S2(f"C2_{i}", [128, D], BF16) for i in range(2)]
    bnt2 = [S2(f"bnt2_{i}", [128, 12], F32) for i in range(2)]
    ymv = [S2(f"ymv_{i}", [128, 2], F32) for i in range(2)]
    ylnv = [S2(f"ylnv_{i}", [128, 1], F32) for i in range(2)]
    yrstd = [S2(f"yrstd_{i}", [128, 1], F32) for i in range(2)]
    ynmr = [S2(f"ynmr_{i}", [128, 1], F32) for i in range(2)]

    pes.close()
    P2es = ExitStack()
    gu_ps = [P2es.enter_context(nc.psum_tensor(f"gu_ps{i}", [128, 512], F32))
             for i in range(3)]
    ffn_ps = [P2es.enter_context(nc.psum_tensor(f"ffn_ps{i}", [128, 1024], F32))
              for i in range(2)]


    way_xnT2s = [0] * (2 * NT2)
    way_combx = [0] * (2 * NT2)
    way_hdn = {}
    way_silu = {}
    way_down = [[0, 0] for _ in range(NT2)]
    way_yexp = [[0, 0] for _ in range(NT2)]
    way_c8 = [[0, 0] for _ in range(NT2)]
    way_fin = [[0, 0] for _ in range(NT2)]
    way_ynmr = [[0, 0] for _ in range(NT2)]
    way_aggr2 = [0, 0]
    lx_x2 = [0] * (2 * NT2)
    lx_out = [0] * (2 * NT2)
    sa_xn2 = [0] * (2 * NT2)
    sp_tr2 = [0] * (2 * NT2)
    S2SEM = ("S2A", "S2B", "S2C", "S2D")

    S1END = way_csg[NT1 - 1]

    def prep(tau):
        for s in range(2):
            g = 2 * tau + s
            w = {"SP": S1END, "LWB": 16}
            if g >= 4:
                w["SV"] = way_combx[g - 4]
            lx_x2[g] = sc.dma("sync", lambda e, g=g: e.dma_start(
                out=x2[g % 4][:, :], in_=x_h[g * 128:(g + 1) * 128, :]),
                S2SEM[g % 4], waits=w)
            w = {S2SEM[g % 4]: lx_x2[g]}
            if g >= 1:
                w["SP"] = max(S1END, sp_tr2[g - 1])
            sa_xn2[g] = sc.act(lambda e, g=g: e.activation(
                xn2[:, :], x2[g % 4][:, :], AF.Identity,
                bias=negmur_all[:, g:g + 1], scale=rstd_all[:, g:g + 1]), waits=w)
            w = {"SA": sa_xn2[g], "LXM": LXM_T}
            if g >= 1:
                w["SV"] = way_xnT2s[g - 1]
            for kc in range(KC):
                sp_tr2[g] = sc.pe(lambda e, kc=kc: e.transpose(
                    tr_ps[:, kc * 128:(kc + 1) * 128],
                    xn2[:, kc * 128:(kc + 1) * 128], idn[:, :]),
                    waits=w if kc == 0 else None, group_end=(kc == KC - 1))
            way_xnT2s[g] = sc.dve(lambda e, s=s: e.tensor_copy(
                xnT2[:, :].rearrange("q (kc n) -> q kc n", kc=KC)
                [:, :, s * 128:(s + 1) * 128],
                tr_ps[:, :].rearrange("q (kc n) -> q kc n", kc=KC)),
                waits={"SP": sp_tr2[g]})

    def emit_down(tau, hc):
        hp = hc % 3
        w = {"SV": way_hdn[(tau, hc)], "LWD": 16}
        if hc == 0 and tau >= 1:
            w["SV"] = max(w["SV"], way_combx[2 * tau - 1])
        for s in range(2):
            for half in range(2):
                last = hc == HC - 1 and half == 1
                r = sc.pe(lambda e, s=s, half=half, hc=hc, hp=hp: e.matmul(
                    ffn_ps[s][:, half * 512:(half + 1) * 512],
                    hdn[hp][:, s * 128:(s + 1) * 128],
                    wdT[:, hc * D + half * 512: hc * D + (half + 1) * 512],
                    start=(hc == 0), stop=(hc == HC - 1), skip_group_check=True),
                    waits=w if (s == 0 and half == 0) else None, group_end=last)
                if last:
                    way_down[tau][s] = r

    fbox = {}

    def emit_post_collective():
        sv_sr = sc.dve(lambda e: e.reciprocal(sall_r[:, :], sall[:, :]),
                       waits={"LWC": 32})
        wpe = {"SV": max(sv_sr, way_hdn[(0, HC - 1)]),
               "SA": way_silu[(0, HC - 1)]}
        sp_rs = sc.pe(lambda e: e.matmul(gu_ps[2][:, 0:8], ones_row[0:1, :],
                                         sall_r[0:1, :], start=True, stop=True,
                                         skip_group_check=True), waits=wpe)
        sv_rsb = sc.dve(lambda e: e.tensor_copy(rs_b[:, :], gu_ps[2][:, 0:8]),
                        waits={"SP": sp_rs})
        sc.dve(lambda e: e.tensor_mul(ct1[:, :], ctall[:, :], rs_b[:, :]),
               waits={"SV": sv_rsb, "LWC": 32})
        sv_ct2 = sc.dve(lambda e: e.tensor_scalar(
            ct2[:, :], ct1[:, :], recip_a[:, :], None, ALU.mult),
            waits={"SV": sv_recipa})
        sa_ctbf = sc.act(lambda e: e.activation(ct_bf[:, :], ct2[:, :],
                                                AF.Copy), waits={"SV": sv_ct2})
        sp_fh = [0] * H

        def wf_load(h, w=None):
            return sc.dma("gpsimd", lambda e, h=h: e.dma_start(
                out=wfc[h % 2][:, :], in_=wfT_h[h * 128:(h + 1) * 128, :]),
                ("LWF", "LWF2")[h % 2], waits=w)

        wf_load(0)
        wf_load(1)
        for h in range(H):
            w = {("LWF", "LWF2")[h % 2]: 16 * (h // 2 + 1), "SA": sa_ctbf}
            if h == 0:
                w["SV"] = max(sv_ct2, way_hdn[(0, HC - 1)])
                w["SA"] = max(sa_ctbf, way_silu[(0, HC - 1)])
            for half in range(2):
                sp_fh[h] = sc.pe(lambda e, half=half, h=h: e.matmul(
                    gu_ps[half][0:1, 0:512], ct_bf[:, h:h + 1],
                    wfc[h % 2][:, half * 512:(half + 1) * 512],
                    start=(h == 0), stop=(h == H - 1), skip_group_check=True),
                    waits=w if half == 0 else None, group_end=(half == 1))
            if h + 2 < H:
                wf_load(h + 2, w={"SP": sp_fh[h]})
        sv_fev = 0
        for half in range(2):
            sv_fev = sc.dve(lambda e, half=half: e.tensor_copy(
                F_b[0:1, half * 512:(half + 1) * 512],
                gu_ps[half][0:1, 0:512]), waits={"SP": sp_fh[H - 1]})
        sp_fb = [0, 0]
        for half in range(2):
            sp_fb[half] = sc.pe(lambda e, half=half: e.matmul(
                gu_ps[half][:, 0:512], ones_row[0:1, :],
                F_b[0:1, half * 512:(half + 1) * 512], start=True, stop=True,
                skip_group_check=True),
                waits={"SV": sv_fev} if half == 0 else None, group_end=True)
        sv_fb = 0
        for half in range(2):
            sv_fb = sc.dve(lambda e, half=half: e.tensor_copy(
                F_b[:, half * 512:(half + 1) * 512], gu_ps[half][:, 0:512]),
                waits={"SP": sp_fb[half]})
        fbox["sv_fb"] = sv_fb

    prep(0)
    for tau in range(NT2):
        for hc in range(HC):
            hp = hc % 3
            gp = hc % 3
            w = {"SV": way_xnT2s[2 * tau + 1], "LWG": 32}
            if tau == 1 and hc == 0:
                w["SV"] = max(w["SV"], fbox["sv_fb"])
            if hc >= 3:
                w["SV"] = max(w["SV"], way_hdn[(tau, hc - 3)])
                w["SA"] = way_silu[(tau, hc - 3)]
            elif tau >= 1:
                w["SV"] = max(w["SV"], way_hdn[(tau - 1, HC - 3 + hc)])
                w["SA"] = way_silu[(tau - 1, HC - 3 + hc)]
            for kc in range(KC):
                sc.pe(lambda e, kc=kc, hc=hc, gp=gp: e.matmul(
                    gu_ps[gp][:, 0:256],
                    wgT[:, kc * HID + hc * 128: kc * HID + (hc + 1) * 128],
                    xnT2[:, kc * 256:(kc + 1) * 256],
                    start=(kc == 0), stop=(kc == KC - 1), skip_group_check=True),
                    waits=w if kc == 0 else None, group_end=False)
            sp_gu = 0
            for kc in range(KC):
                sp_gu = sc.pe(lambda e, kc=kc, hc=hc, gp=gp: e.matmul(
                    gu_ps[gp][:, 256:512],
                    wuT[:, kc * HID + hc * 128: kc * HID + (hc + 1) * 128],
                    xnT2[:, kc * 256:(kc + 1) * 256],
                    start=False, stop=(kc == KC - 1), skip_group_check=True),
                    group_end=(kc == KC - 1))
            way_silu[(tau, hc)] = sc.act(lambda e, gp=gp, hp=hp: e.activation(
                silu_sb[hp][:, :], gu_ps[gp][:, 0:256], AF.Silu, scale=S127),
                waits={"SP": sp_gu})
            way_hdn[(tau, hc)] = sc.dve(lambda e, gp=gp, hp=hp: e.tensor_mul(
                hdn[hp][:, :], gu_ps[gp][:, 256:512], silu_sb[hp][:, :]),
                waits={"SA": way_silu[(tau, hc)]})
            if hc >= 2:
                emit_down(tau, hc - 2)
        emit_down(tau, HC - 2)
        emit_down(tau, HC - 1)
        if tau == 0:
            emit_post_collective()

        if tau + 1 < NT2:
            prep(tau + 1)

        for s in range(2):
            g = 2 * tau + s
            w = {S2SEM[g % 4]: lx_x2[g]}
            if tau >= 1:
                w["SL"] = way_fin[tau - 1][s]
            sc.dve(lambda e, g=g, s=s: e.tensor_scalar(
                C2[s][:, :], x2[g % 4][:, :], rstd_all[:, g:g + 1],
                negmur_all[:, g:g + 1], ALU.mult, ALU.add), waits=w)
            w = {"SV": fbox["sv_fb"]}
            if tau >= 1:
                w[("LXOA", "LXOB")[s]] = lx_out[g - 2]
                w["SL"] = way_c8[tau - 1][s]
            sc.dve(lambda e, s=s: e.tensor_add(B2[s][:, :], C2[s][:, :],
                                               F_b[:, :]), waits=w)
            w = {}
            if tau >= 1:
                w["SA"] = way_yexp[tau - 1][s]
                w["SL"] = way_ynmr[tau - 1][s]
            sc.dve(lambda e, s=s: e.bn_stats(bnt2[s][:, 0:6], B2[s][:, 0:512]),
                   waits=w)
            sc.dve(lambda e, s=s: e.bn_stats(bnt2[s][:, 6:12], B2[s][:, 512:1024]))
            way_aggr2[s] = sc.dve(lambda e, s=s: e.bn_aggr(ymv[s][:, :],
                                                           bnt2[s][:, :]))
        for s in range(2):
            g = 2 * tau + s
            w = {"SP": way_down[tau][s]}
            if tau >= 1:
                w["SL"] = way_fin[tau - 1][s]
            way_combx[g] = sc.dve(lambda e, g=g, s=s: e.scalar_tensor_tensor(
                A2[s][:, :], ffn_ps[s][:, 0:1024], S127 * S127, x2[g % 4][:, :],
                ALU.mult, ALU.add), waits=w)

        for s in range(2):
            g = 2 * tau + s
            w = {"SV": way_aggr2[s]}
            sa_l = sc.act(lambda e, s=s: e.activation(
                ylnv[s][:, :], ymv[s][:, 1:2], AF.Ln, bias=eps_col[:, :]),
                waits=w)
            we = {}
            if tau >= 1:
                we["SL"] = way_c8[tau - 1][s]
            way_yexp[tau][s] = sc.act(lambda e, s=s: e.activation(
                yrstd[s][:, :], ylnv[s][:, :], AF.Exp, scale=-0.5), waits=we)
            way_ynmr[tau][s] = sc.pool(lambda e, s=s: e.tensor_scalar(
                ynmr[s][:, :], ymv[s][:, 0:1], yrstd[s][:, :], -1.0,
                ALU.mult, ALU.mult),
                waits={"SA": way_yexp[tau][s], "SV": fbox["sv_fb"]})
            way_c8[tau][s] = sc.pool(lambda e, s=s: e.tensor_scalar(
                C2[s][:, :], B2[s][:, :], yrstd[s][:, :], ynmr[s][:, :],
                ALU.mult, ALU.add), waits={"SV": way_aggr2[s]})
            sc.pool(lambda e, s=s: e.tensor_mul(B2[s][:, :], C2[s][:, :],
                                                gam_b[:, :]))
            sc.pool(lambda e, s=s: e.tensor_add(C2[s][:, :], B2[s][:, :],
                                                bet_b[:, :]))
            sl_fin = sc.pool(lambda e, s=s: e.tensor_add(
                B2[s][:, :], A2[s][:, :], C2[s][:, :]),
                waits={"SV": way_combx[g]})
            way_fin[tau][s] = sl_fin
            lx_out[g] = sc.dma("sync", lambda e, g=g, s=s: e.dma_start(
                out=out_h[g * 128:(g + 1) * 128, :], in_=B2[s][:, :]),
                ("LXOA", "LXOB")[s], waits={"SL": sl_fin})

    with nc.Block() as block:
        emit_plan(sc, block, sems)

    S2es.close()
    s1bes.close()
    P2es.close()
    sems_es.close()
    es.close()
    return nc


def kernel(L0_attn_h, L0_attn_Wl, L0_attn_Wr, L0_attn_att_l, L0_attn_att_r,
           L0_attn_Wf, L0_attn_ln_gamma, L0_attn_ln_beta, L0_ffn_w_gate,
           L0_ffn_w_up, L0_ffn_w_down):
    x = np.asarray(L0_attn_h, dtype=np.float32)
    B, N, _ = x.shape
    T = (B * N) // N_CORES
    nc = build_full(T)
    in_maps = make_in_maps(x, L0_attn_Wr, L0_attn_Wf, L0_attn_att_r,
                           L0_attn_ln_gamma, L0_attn_ln_beta, L0_ffn_w_gate,
                           L0_ffn_w_up, L0_ffn_w_down)
    res = run_bass_kernel_spmd(nc, in_maps, core_ids=list(range(N_CORES)))
    out = np.concatenate([np.asarray(r["out"]) for r in res.results], axis=0)
    return out.reshape(B, N, D).astype(np.float32)


def make_in_maps(x, Wr, Wf, att_r, gam, bet, wg, wu, wd):
    B, N, _ = x.shape
    T = (B * N) // N_CORES
    wrT = np.ascontiguousarray(np.asarray(Wr, np.float32).T)
    wfT = np.ascontiguousarray(np.asarray(Wf, np.float32).T)
    wgT = np.ascontiguousarray(np.asarray(wg, np.int32).T)
    wuT = np.ascontiguousarray(np.asarray(wu, np.int32).T)
    wdT = np.ascontiguousarray(np.asarray(wd, np.int32).T)
    attr = np.asarray(att_r, np.float32)
    gam = np.ascontiguousarray(np.asarray(gam, np.float32).reshape(1, D))
    bet = np.ascontiguousarray(np.asarray(bet, np.float32).reshape(1, D))
    idn = np.eye(128, dtype=np.float32).astype(ml_dtypes.bfloat16)
    ones_bf = np.ones((128, 1), np.float32).astype(ml_dtypes.bfloat16)
    ones_row = np.ones((1, 128), np.float32)
    ones_rowb = np.ones((1, 128), np.float32).astype(ml_dtypes.bfloat16)
    xf = np.asarray(x, np.float32).reshape(B * N, D)
    in_maps = []
    for k in range(N_CORES):
        in_maps.append({
            "x": np.ascontiguousarray(xf[k * T:(k + 1) * T, :]),
            "wrT": wrT, "wfT": wfT, "wgT": wgT, "wuT": wuT, "wdT": wdT,
            "attr_row": np.ascontiguousarray(attr.reshape(1, HD)),
            "attr_col": np.ascontiguousarray(attr.reshape(HD, 1)),
            "gam": gam, "bet": bet, "idn": idn, "ones_bf": ones_bf,
            "ones_row": ones_row, "ones_rowb": ones_rowb,
        })
    return in_maps


# version 11
# speedup vs baseline: 1.0275x; 1.0275x over previous
import numpy as np
import ml_dtypes
from contextlib import ExitStack

import concourse.bass as bass
import concourse.mybir as mybir
from concourse.bass_utils import run_bass_kernel_spmd

F32 = mybir.dt.float32
BF16 = mybir.dt.bfloat16
I32 = mybir.dt.int32
AF = mybir.ActivationFunctionType
ALU = mybir.AluOpType
AX = mybir.AxisListType

D = 1024
H = 8
HD = 128
HID = 3072
KC = D // 128
HC = HID // 128
S127 = 1.0 / 127.0
EPS = 1e-5
N_CORES = 8

SYNC_SEMS = ("LXM", "S1A", "S1B", "S1C", "S2A", "S2B", "S2C", "S2D",
             "LXOA", "LXOB")
GP_SEMS = ("LWA", "LWR", "LWG", "LWD", "LWF", "LWF2", "LWB", "LWC", "LWGB")
ALL_SEMS = SYNC_SEMS + GP_SEMS + ("CC", "SA", "SV", "SP", "SL")


class Sched:

    def __init__(self):
        self.plan = []
        self.cnt = {k: 0 for k in ALL_SEMS}

    def op(self, engine, fn, waits=None, inc=None):
        waits = {k: v for k, v in (waits or {}).items() if v and v > 0}
        incs_abs = []
        for sem, amount in (inc or []):
            self.cnt[sem] += amount
            incs_abs.append((sem, amount, self.cnt[sem]))
        self.plan.append((engine, fn, waits, incs_abs))
        return {s: a for s, _, a in incs_abs}

    def dma(self, engine, fn, sem, waits=None):
        return self.op(engine, fn, waits, inc=[(sem, 16)])[sem]

    def act(self, fn, waits=None):
        w = dict(waits or {})
        w["SA"] = max(w.get("SA", 0), self.cnt["SA"])
        return self.op("scalar", fn, w, inc=[("SA", 1)])["SA"]

    def dve(self, fn, waits=None):
        w = dict(waits or {})
        w["SV"] = max(w.get("SV", 0), self.cnt["SV"])
        return self.op("vector", fn, w, inc=[("SV", 1)])["SV"]

    def pool(self, fn, waits=None):
        w = dict(waits or {})
        w["SL"] = max(w.get("SL", 0), self.cnt["SL"])
        return self.op("gpsimd", fn, w, inc=[("SL", 1)])["SL"]

    def pe(self, fn, waits=None, group_end=True):
        if group_end:
            return self.op("tensor", fn, waits, inc=[("SP", 1)])["SP"]
        self.op("tensor", fn, waits)
        return None


def emit_plan(sc, block, sems):
    def emit(engine_name, eng):
        high = {}
        for (en, fn, waits, incs) in sc.plan:
            if en != engine_name:
                continue
            for s, v in sorted(waits.items()):
                if high.get(s, 0) >= v:
                    continue
                eng.wait_ge(sems[s], v)
                high[s] = v
            inst = fn(eng)
            for s, amount, absval in incs:
                inst = inst.then_inc(sems[s], amount)

    @block.sync
    def _(sync):
        emit("sync", sync)
        for s in SYNC_SEMS:
            if sc.cnt[s] > 0:
                sync.wait_ge(sems[s], sc.cnt[s])

    @block.gpsimd
    def _(g):
        emit("gpsimd", g)
        for s in GP_SEMS:
            if sc.cnt[s] > 0:
                g.wait_ge(sems[s], sc.cnt[s])

    @block.scalar
    def _(s):
        emit("scalar", s)

    @block.vector
    def _(v):
        emit("vector", v)

    @block.tensor
    def _(t):
        emit("tensor", t)


def build_full(T):
    NT1 = T // 128
    NT2 = T // 256
    nc = bass.Bass("TRN2", target_bir_lowering=False, debug=False)

    x_h = nc.dram_tensor("x", [T, D], F32, kind="ExternalInput").ap()
    wrT_h = nc.dram_tensor("wrT", [D, D], F32, kind="ExternalInput").ap()
    wfT_h = nc.dram_tensor("wfT", [D, D], F32, kind="ExternalInput").ap()
    wgT_h = nc.dram_tensor("wgT", [D, HID], I32, kind="ExternalInput").ap()
    wuT_h = nc.dram_tensor("wuT", [D, HID], I32, kind="ExternalInput").ap()
    wdT_h = nc.dram_tensor("wdT", [HID, D], I32, kind="ExternalInput").ap()
    attr_row_h = nc.dram_tensor("attr_row", [1, HD], F32, kind="ExternalInput").ap()
    attr_col_h = nc.dram_tensor("attr_col", [HD, 1], F32, kind="ExternalInput").ap()
    gam_h = nc.dram_tensor("gam", [1, D], F32, kind="ExternalInput").ap()
    bet_h = nc.dram_tensor("bet", [1, D], F32, kind="ExternalInput").ap()
    idn_h = nc.dram_tensor("idn", [128, 128], BF16, kind="ExternalInput").ap()
    ones_bf_h = nc.dram_tensor("ones_bf", [128, 1], BF16, kind="ExternalInput").ap()
    ones_row_h = nc.dram_tensor("ones_row", [1, 128], F32, kind="ExternalInput").ap()
    ones_rowb_h = nc.dram_tensor("ones_rowb", [1, 128], BF16, kind="ExternalInput").ap()
    out_h = nc.dram_tensor("out", [T, D], F32, kind="ExternalOutput").ap()
    bounce_in = nc.dram_tensor("bounce_in", [128, 16], F32)
    bounce_out = nc.dram_tensor("bounce_out", [128, 16], F32)

    es = ExitStack()
    SBp = lambda n, s, d: es.enter_context(nc.sbuf_tensor(n, s, d))

    wgT = SBp("wgT_sb", [128, KC * HID], BF16)
    wuT = SBp("wuT_sb", [128, KC * HID], BF16)
    wdT = SBp("wdT_sb", [128, HC * D], BF16)
    idn = SBp("idn_sb", [128, 128], BF16)
    ones_bf = SBp("ones_bf_sb", [128, 1], BF16)
    ones_row = SBp("ones_row_sb", [1, 128], F32)
    ones_rowb = SBp("ones_rowb_sb", [1, 128], BF16)
    attr_col = SBp("attr_col_sb", [HD, 1], F32)
    recip_a = SBp("recip_a", [HD, 1], F32)
    rstd_all = SBp("rstd_all", [128, NT1], F32)
    negmur_all = SBp("negmur_all", [128, NT1], F32)
    ct_bf = SBp("ct_bf", [128, H], BF16)
    eps_col = SBp("eps_col", [128, 1], F32)
    gam_b = SBp("gam_b", [128, D], BF16)
    bet_b = SBp("bet_b", [128, D], BF16)

    tr_ps = es.enter_context(nc.psum_tensor("tr_ps", [128, 1024], BF16))
    pes = ExitStack()
    fr_ps = [pes.enter_context(nc.psum_tensor(f"fr_ps{i}", [128, 1024], F32))
             for i in range(2)]
    ct_ps = pes.enter_context(nc.psum_tensor("ct_ps", [128, 512], F32))
    bc_ps = pes.enter_context(nc.psum_tensor("bc_ps", [128, 1024], F32))

    s1bes = ExitStack()
    S1B = lambda n, s, d: s1bes.enter_context(nc.sbuf_tensor(n, s, d))
    wfc = [S1B(f"wfc_{i}", [128, D], BF16) for i in range(2)]
    ctall = S1B("ctall", [128, 8], F32)
    sall = S1B("sall", [1, 8], F32)
    sall_r = S1B("sall_r", [1, 8], F32)
    rs_b = S1B("rs_b", [128, 8], F32)
    ct1 = S1B("ct1", [128, 8], F32)
    ct2 = S1B("ct2", [128, 8], F32)

    s1es = ExitStack()
    S1 = lambda n, s, d: s1es.enter_context(nc.sbuf_tensor(n, s, d))
    wrT_s = S1("wrT_s", [128, KC * D], BF16)
    attr_rep = S1("attr_rep", [128, D], BF16)
    x1 = [S1(f"x1_{i}", [128, D], F32) for i in range(3)]
    xn1 = [S1(f"xn1_{i}", [128, D], BF16) for i in range(2)]
    xnT1 = [S1(f"xnT1_{i}", [128, D], BF16) for i in range(2)]
    frbf = [S1(f"frbf_{i}", [128, D], BF16) for i in range(2)]
    bnt1 = [S1(f"bnt1_{i}", [128, 12], F32) for i in range(2)]
    mv1 = [S1(f"mv1_{i}", [128, 2], F32) for i in range(2)]
    lnv1 = [S1(f"lnv1_{i}", [128, 1], F32) for i in range(2)]
    srab = [S1(f"srab_{i}", [128, 16], F32) for i in range(2)]
    srs = [S1(f"srs_{i}", [128, 8], F32) for i in range(2)]
    e_bf = [S1(f"e_bf_{i}", [128, 8], BF16) for i in range(2)]
    cs_stage = S1("cs_stage", [128, 16], F32)

    sems_es = ExitStack()
    sems = {k: sems_es.enter_context(nc.semaphore(k)) for k in ALL_SEMS}
    sc = Sched()

    sc.dma("sync", lambda e: e.dma_start(out=idn[:, :], in_=idn_h), "LXM")
    sc.dma("sync", lambda e: e.dma_start(out=ones_bf[:, :], in_=ones_bf_h), "LXM")
    sc.dma("sync", lambda e: e.dma_start(out=ones_row[:, :], in_=ones_row_h), "LXM")
    sc.dma("sync", lambda e: e.dma_start(out=ones_rowb[:, :], in_=ones_rowb_h), "LXM")
    sc.dma("sync", lambda e: e.dma_start(out=attr_col[:, :], in_=attr_col_h), "LXM")
    LXM_T = sc.cnt["LXM"]

    for c in range(KC):
        sc.dma("gpsimd", lambda e, c=c: e.dma_start(
            out=attr_rep[0:1, c * HD:(c + 1) * HD], in_=attr_row_h), "LWA")
    LWA_T = sc.cnt["LWA"]
    sc.dma("gpsimd", lambda e: e.dma_start(out=gam_b[0:1, :], in_=gam_h), "LWGB")
    sc.dma("gpsimd", lambda e: e.dma_start(out=bet_b[0:1, :], in_=bet_h), "LWGB")
    sc.dma("gpsimd", lambda e: e.dma_start(
        out=wrT_s[:, :].rearrange("p (kc n) -> p kc n", kc=KC),
        in_=wrT_h.rearrange("(kc p) n -> p kc n", p=128)), "LWR")
    sc.dma("gpsimd", lambda e: e.dma_start(
        out=wgT[:, :].rearrange("p (kc n) -> p kc n", kc=KC),
        in_=wgT_h.rearrange("(kc p) n -> p kc n", p=128)), "LWG")
    sc.dma("gpsimd", lambda e: e.dma_start(
        out=wuT[:, :].rearrange("p (kc n) -> p kc n", kc=KC),
        in_=wuT_h.rearrange("(kc p) n -> p kc n", p=128)), "LWG")
    sc.dma("gpsimd", lambda e: e.dma_start(
        out=wdT[:, :].rearrange("p (hc n) -> p hc n", hc=HC),
        in_=wdT_h.rearrange("(hc p) n -> p hc n", p=128)), "LWD")

    sv_eps = sc.dve(lambda e: e.memset(eps_col[:, :], EPS))
    sc.dve(lambda e: e.memset(cs_stage[:, :], 0.0))
    sv_recipa = sc.dve(lambda e: e.reciprocal(recip_a[:, :], attr_col[:, :]),
                       waits={"LXM": LXM_T})

    def bcast_bf(row_tile, w0):
        sp = 0
        for half in range(2):
            sp = sc.pe(lambda e, half=half: e.matmul(
                bc_ps[:, half * 512:(half + 1) * 512], ones_rowb[0:1, :],
                row_tile[0:1, half * 512:(half + 1) * 512], start=True,
                stop=True, skip_group_check=True),
                waits=w0 if half == 0 else None, group_end=(half == 1))
        return sc.dve(lambda e: e.tensor_copy(row_tile[:, :], bc_ps[:, 0:1024]),
                      waits={"SP": sp})

    sv_attr = bcast_bf(attr_rep, {"LWA": LWA_T, "LXM": LXM_T})
    sv_gb = bcast_bf(gam_b, {"LWGB": 32, "SV": sv_attr})
    sv_bb = bcast_bf(bet_b, {"LWGB": 32, "SV": sv_gb})

    sv_wrs = 0
    for c in range(KC):
        sc.dve(lambda e, c=c: e.tensor_mul(frbf[0][:, :],
                                           wrT_s[:, c * D:(c + 1) * D],
                                           attr_rep[:, :]),
               waits={"SV": sv_attr, "LWR": 16})
        sv_wrs = sc.dve(lambda e, c=c: e.tensor_copy(wrT_s[:, c * D:(c + 1) * D],
                                                     frbf[0][:, :]))


    way_xn = [0] * NT1
    way_stats = [0] * NT1
    way_evac = [0] * NT1
    way_red = [0] * NT1
    way_frbf = [0] * NT1
    way_trg = [0] * NT1
    way_frg = [0] * NT1
    way_csg = [0] * NT1
    way_e = [0] * NT1
    way_srs = [0] * NT1
    lx_x1 = [0] * NT1
    S1SEM = ("S1A", "S1B", "S1C")

    def s1_load(t):
        w = {}
        if t >= 3:
            w = {"SA": way_xn[t - 3], "SV": way_stats[t - 3]}
        lx_x1[t] = sc.dma("sync", lambda e, t=t: e.dma_start(
            out=x1[t % 3][:, :], in_=x_h[t * 128:(t + 1) * 128, :]),
            S1SEM[t % 3], waits=w)

    def s1_stats(t):
        p = t % 2
        w = {S1SEM[t % 3]: lx_x1[t]}
        if t >= 2:
            w["SA"] = way_xn[t - 2]
        sc.dve(lambda e, t=t, p=p: e.bn_stats(bnt1[p][:, 0:6],
                                              x1[t % 3][:, 0:512]), waits=w)
        sc.dve(lambda e, t=t, p=p: e.bn_stats(bnt1[p][:, 6:12],
                                              x1[t % 3][:, 512:1024]))
        way_stats[t] = sc.dve(lambda e, p=p: e.bn_aggr(mv1[p][:, :],
                                                       bnt1[p][:, :]))

    def s1_lnxn(t):
        p = t % 2
        sa_ln = sc.act(lambda e, p=p: e.activation(
            lnv1[p][:, :], mv1[p][:, 1:2], AF.Ln, bias=eps_col[:, :]),
            waits={"SV": max(way_stats[t], sv_eps)})
        sa_rstd = sc.act(lambda e, t=t, p=p: e.activation(
            rstd_all[:, t:t + 1], lnv1[p][:, :], AF.Exp, scale=-0.5))
        sl_nmr = sc.pool(lambda e, t=t, p=p: e.tensor_scalar(
            negmur_all[:, t:t + 1], mv1[p][:, 0:1], rstd_all[:, t:t + 1], -1.0,
            ALU.mult, ALU.mult), waits={"SA": sa_rstd})
        w = {"SL": sl_nmr, S1SEM[t % 3]: lx_x1[t]}
        if t >= 2:
            w["SP"] = way_trg[t - 2]
        way_xn[t] = sc.act(lambda e, p=p, t=t: e.activation(
            xn1[p][:, :], x1[t % 3][:, :], AF.Identity,
            bias=negmur_all[:, t:t + 1], scale=rstd_all[:, t:t + 1]), waits=w)

    def s1_cs(t):
        p = t % 2
        w = {"SA": max(way_e[t], way_frbf[t]), "LXM": LXM_T}
        for h in range(H):
            sc.pe(lambda e, h=h, p=p, t=t: e.matmul(
                ct_ps[:, h:h + 1], frbf[p][:, h * HD:(h + 1) * HD],
                e_bf[p][:, h:h + 1], start=(t == 0 and h == 0),
                stop=(t == NT1 - 1), skip_group_check=True),
                waits=w if h == 0 else None, group_end=False)
        way_csg[t] = sc.pe(lambda e, p=p, t=t: e.matmul(
            ct_ps[0:1, 8:16], ones_bf[:, :], e_bf[p][:, :],
            start=False, stop=(t == NT1 - 1), skip_group_check=True))

    def s1_tr(t):
        p = t % 2
        w = {"SA": way_xn[t], "LXM": LXM_T}
        if t >= 1:
            w["SV"] = way_evac[t - 1]
        for c in range(KC):
            way_trg[t] = sc.pe(lambda e, c=c, p=p: e.transpose(
                tr_ps[:, c * 128:(c + 1) * 128], xn1[p][:, c * 128:(c + 1) * 128],
                idn[:, :]), waits=w if c == 0 else None, group_end=(c == KC - 1))
        way_evac[t] = sc.dve(lambda e, t=t: e.tensor_copy(
            xnT1[t % 2][:, :], tr_ps[:, :]), waits={"SP": way_trg[t]})

    def s1_fr(t):
        p = t % 2
        w = {"SV": max(way_evac[t], sv_wrs)}
        if t >= 2:
            w["SA"] = max(w.get("SA", 0), way_frbf[t - 2])
        for half in range(2):
            for kc in range(KC):
                way_frg[t] = sc.pe(lambda e, half=half, kc=kc, p=p, t=t: e.matmul(
                    fr_ps[p][:, half * 512:(half + 1) * 512],
                    xnT1[t % 2][:, kc * 128:(kc + 1) * 128],
                    wrT_s[:, kc * D + half * 512: kc * D + (half + 1) * 512],
                    start=(kc == 0), stop=(kc == KC - 1), skip_group_check=True),
                    waits=w if (half == 0 and kc == 0) else None,
                    group_end=(half == 1 and kc == KC - 1))

    def s1_post(t):
        p = t % 2
        w = {"SP": way_frg[t]}
        if t >= 2:
            w["SP"] = max(w["SP"], way_csg[t - 2])
            w["SV"] = way_red[t - 2]
        way_frbf[t] = sc.act(lambda e, p=p: e.activation(
            frbf[p][:, :], fr_ps[p][:, :], AF.Copy), waits=w)
        sc.dve(lambda e, p=p: e.reduce_sum(
            srab[p][:, 0:8],
            frbf[p][:, :].rearrange("q (h j) -> q h j", h=H), axis=AX.X),
            waits={"SA": way_frbf[t]})
        way_red[t] = sc.dve(lambda e, p=p: e.reduce_sum(
            srab[p][:, 8:16],
            frbf[p][:, :].rearrange("q (h j) -> q h j", h=H), axis=AX.X,
            apply_absolute_value=True))
        way_srs[t] = sc.dve(lambda e, p=p: e.scalar_tensor_tensor(
            srs[p][:, :], srab[p][:, 0:8], 0.505 / 0.495, srab[p][:, 8:16],
            ALU.mult, ALU.add))
        w = {"SV": way_srs[t]}
        if t >= 2:
            w["SP"] = max(w.get("SP", 0), way_csg[t - 2])
        way_e[t] = sc.act(lambda e, p=p: e.activation(
            e_bf[p][:, :], srs[p][:, :], AF.Exp, scale=0.495), waits=w)

    for i in range(NT1 + 4):
        if i < NT1:
            s1_load(i)
        if 1 <= i <= NT1:
            s1_stats(i - 1)
            s1_lnxn(i - 1)
            s1_tr(i - 1)
        if 2 <= i <= NT1 + 1:
            s1_fr(i - 2)
        if 3 <= i <= NT1 + 2:
            s1_post(i - 3)
        if 4 <= i <= NT1 + 3:
            s1_cs(i - 4)

    sv_ct = sc.dve(lambda e: e.tensor_copy(cs_stage[:, 0:8], ct_ps[:, 0:8]),
                   waits={"SP": way_csg[NT1 - 1]})
    sv_s = sc.dve(lambda e: e.tensor_copy(cs_stage[0:1, 8:16], ct_ps[0:1, 8:16]))
    sc.dma("gpsimd", lambda e: e.dma_start(
        out=bounce_in.ap()[:, :], in_=cs_stage[:, :]), "LWB", waits={"SV": sv_s})
    sc.op("gpsimd", lambda e: e.collective_compute(
        "AllReduce", ALU.add, replica_groups=[[0, 1, 2, 3], [4, 5, 6, 7]],
        ins=[bounce_in.ap().opt()], outs=[bounce_out.ap().opt()]),
        waits={"LWB": 16}, inc=[("CC", 1)])
    sc.dma("gpsimd", lambda e: e.dma_start(
        out=ctall[:, :], in_=bounce_out.ap()[:, 0:8]), "LWC", waits={"CC": 1})
    sc.dma("gpsimd", lambda e: e.dma_start(
        out=sall[:, :], in_=bounce_out.ap()[0:1, 8:16]), "LWC")


    s1es.close()
    S2es = ExitStack()
    S2 = lambda n, s, d: S2es.enter_context(nc.sbuf_tensor(n, s, d))
    F_b = S2("F_b", [128, D], F32)
    x2 = [S2(f"x2_{i}", [128, D], F32) for i in range(4)]
    xn2 = S2("xn2", [128, D], BF16)
    xnT2 = S2("xnT2", [128, KC * 256], BF16)
    hdn = [S2(f"hdn_{i}", [128, 256], BF16) for i in range(3)]
    silu_sb = [S2(f"silu_{i}", [128, 256], BF16) for i in range(3)]
    A2 = [S2(f"A2_{i}", [128, D], F32) for i in range(2)]
    B2 = [S2(f"B2_{i}", [128, D], F32) for i in range(2)]
    C2 = [S2(f"C2_{i}", [128, D], BF16) for i in range(2)]
    bnt2 = [S2(f"bnt2_{i}", [128, 12], F32) for i in range(2)]
    ymv = [S2(f"ymv_{i}", [128, 2], F32) for i in range(2)]
    ylnv = [S2(f"ylnv_{i}", [128, 1], F32) for i in range(2)]
    yrstd = [S2(f"yrstd_{i}", [128, 1], F32) for i in range(2)]
    ynmr = [S2(f"ynmr_{i}", [128, 1], F32) for i in range(2)]

    pes.close()
    P2es = ExitStack()
    gu_ps = [P2es.enter_context(nc.psum_tensor(f"gu_ps{i}", [128, 512], F32))
             for i in range(3)]
    ffn_ps = [P2es.enter_context(nc.psum_tensor(f"ffn_ps{i}", [128, 1024], F32))
              for i in range(2)]


    way_xnT2s = [0] * (2 * NT2)
    way_combx = [0] * (2 * NT2)
    way_hdn = {}
    way_silu = {}
    way_down = [[0, 0] for _ in range(NT2)]
    way_yexp = [[0, 0] for _ in range(NT2)]
    way_c8 = [[0, 0] for _ in range(NT2)]
    way_fin = [[0, 0] for _ in range(NT2)]
    way_ynmr = [[0, 0] for _ in range(NT2)]
    way_aggr2 = [0, 0]
    lx_x2 = [0] * (2 * NT2)
    lx_out = [0] * (2 * NT2)
    sa_xn2 = [0] * (2 * NT2)
    sp_tr2 = [0] * (2 * NT2)
    S2SEM = ("S2A", "S2B", "S2C", "S2D")

    S1END = way_csg[NT1 - 1]

    def prep(tau):
        for s in range(2):
            g = 2 * tau + s
            w = {"SP": S1END, "LWB": 16}
            if g >= 4:
                w["SV"] = way_combx[g - 4]
            lx_x2[g] = sc.dma("sync", lambda e, g=g: e.dma_start(
                out=x2[g % 4][:, :], in_=x_h[g * 128:(g + 1) * 128, :]),
                S2SEM[g % 4], waits=w)
            w = {S2SEM[g % 4]: lx_x2[g]}
            if g >= 1:
                w["SP"] = max(S1END, sp_tr2[g - 1])
            sa_xn2[g] = sc.act(lambda e, g=g: e.activation(
                xn2[:, :], x2[g % 4][:, :], AF.Identity,
                bias=negmur_all[:, g:g + 1], scale=rstd_all[:, g:g + 1]), waits=w)
            w = {"SA": sa_xn2[g], "LXM": LXM_T}
            if g >= 1:
                w["SV"] = way_xnT2s[g - 1]
            for kc in range(KC):
                sp_tr2[g] = sc.pe(lambda e, kc=kc: e.transpose(
                    tr_ps[:, kc * 128:(kc + 1) * 128],
                    xn2[:, kc * 128:(kc + 1) * 128], idn[:, :]),
                    waits=w if kc == 0 else None, group_end=(kc == KC - 1))
            way_xnT2s[g] = sc.dve(lambda e, s=s: e.tensor_copy(
                xnT2[:, :].rearrange("q (kc n) -> q kc n", kc=KC)
                [:, :, s * 128:(s + 1) * 128],
                tr_ps[:, :].rearrange("q (kc n) -> q kc n", kc=KC)),
                waits={"SP": sp_tr2[g]})

    def emit_down(tau, hc):
        hp = hc % 3
        w = {"SV": way_hdn[(tau, hc)], "LWD": 16}
        if hc == 0 and tau >= 1:
            w["SV"] = max(w["SV"], way_combx[2 * tau - 1])
        for s in range(2):
            for half in range(2):
                last = hc == HC - 1 and half == 1
                r = sc.pe(lambda e, s=s, half=half, hc=hc, hp=hp: e.matmul(
                    ffn_ps[s][:, half * 512:(half + 1) * 512],
                    hdn[hp][:, s * 128:(s + 1) * 128],
                    wdT[:, hc * D + half * 512: hc * D + (half + 1) * 512],
                    start=(hc == 0), stop=(hc == HC - 1), skip_group_check=True),
                    waits=w if (s == 0 and half == 0) else None, group_end=last)
                if last:
                    way_down[tau][s] = r

    fbox = {}

    def emit_post_collective():
        sv_sr = sc.dve(lambda e: e.reciprocal(sall_r[:, :], sall[:, :]),
                       waits={"LWC": 32})
        wpe = {"SV": max(sv_sr, way_hdn[(0, HC - 1)]),
               "SA": way_silu[(0, HC - 1)]}
        sp_rs = sc.pe(lambda e: e.matmul(gu_ps[2][:, 0:8], ones_row[0:1, :],
                                         sall_r[0:1, :], start=True, stop=True,
                                         skip_group_check=True), waits=wpe)
        sv_rsb = sc.dve(lambda e: e.tensor_copy(rs_b[:, :], gu_ps[2][:, 0:8]),
                        waits={"SP": sp_rs})
        sc.dve(lambda e: e.tensor_mul(ct1[:, :], ctall[:, :], rs_b[:, :]),
               waits={"SV": sv_rsb, "LWC": 32})
        sv_ct2 = sc.dve(lambda e: e.tensor_scalar(
            ct2[:, :], ct1[:, :], recip_a[:, :], None, ALU.mult),
            waits={"SV": sv_recipa})
        sa_ctbf = sc.act(lambda e: e.activation(ct_bf[:, :], ct2[:, :],
                                                AF.Copy), waits={"SV": sv_ct2})
        sp_fh = [0] * H

        def wf_load(h, w=None):
            return sc.dma("gpsimd", lambda e, h=h: e.dma_start(
                out=wfc[h % 2][:, :], in_=wfT_h[h * 128:(h + 1) * 128, :]),
                ("LWF", "LWF2")[h % 2], waits=w)

        wf_load(0)
        wf_load(1)
        for h in range(H):
            w = {("LWF", "LWF2")[h % 2]: 16 * (h // 2 + 1), "SA": sa_ctbf}
            if h == 0:
                w["SV"] = max(sv_ct2, way_hdn[(0, HC - 1)])
                w["SA"] = max(sa_ctbf, way_silu[(0, HC - 1)])
            for half in range(2):
                sp_fh[h] = sc.pe(lambda e, half=half, h=h: e.matmul(
                    gu_ps[half][0:1, 0:512], ct_bf[:, h:h + 1],
                    wfc[h % 2][:, half * 512:(half + 1) * 512],
                    start=(h == 0), stop=(h == H - 1), skip_group_check=True),
                    waits=w if half == 0 else None, group_end=(half == 1))
            if h + 2 < H:
                wf_load(h + 2, w={"SP": sp_fh[h]})
        sv_fev = 0
        for half in range(2):
            sv_fev = sc.dve(lambda e, half=half: e.tensor_copy(
                F_b[0:1, half * 512:(half + 1) * 512],
                gu_ps[half][0:1, 0:512]), waits={"SP": sp_fh[H - 1]})
        sp_fb = [0, 0]
        for half in range(2):
            sp_fb[half] = sc.pe(lambda e, half=half: e.matmul(
                gu_ps[half][:, 0:512], ones_row[0:1, :],
                F_b[0:1, half * 512:(half + 1) * 512], start=True, stop=True,
                skip_group_check=True),
                waits={"SV": sv_fev} if half == 0 else None, group_end=True)
        sv_fb = 0
        for half in range(2):
            sv_fb = sc.dve(lambda e, half=half: e.tensor_copy(
                F_b[:, half * 512:(half + 1) * 512], gu_ps[half][:, 0:512]),
                waits={"SP": sp_fb[half]})
        fbox["sv_fb"] = sv_fb

    prep(0)
    for tau in range(NT2):
        for hc in range(HC):
            hp = hc % 3
            gp = hc % 3
            w = {"SV": way_xnT2s[2 * tau + 1], "LWG": 32}
            if tau == 1 and hc == 0:
                w["SV"] = max(w["SV"], fbox["sv_fb"])
            if hc >= 3:
                w["SV"] = max(w["SV"], way_hdn[(tau, hc - 3)])
                w["SA"] = way_silu[(tau, hc - 3)]
            elif tau >= 1:
                w["SV"] = max(w["SV"], way_hdn[(tau - 1, HC - 3 + hc)])
                w["SA"] = way_silu[(tau - 1, HC - 3 + hc)]
            for kc in range(KC):
                sc.pe(lambda e, kc=kc, hc=hc, gp=gp: e.matmul(
                    gu_ps[gp][:, 0:256],
                    wgT[:, kc * HID + hc * 128: kc * HID + (hc + 1) * 128],
                    xnT2[:, kc * 256:(kc + 1) * 256],
                    start=(kc == 0), stop=(kc == KC - 1), skip_group_check=True),
                    waits=w if kc == 0 else None, group_end=False)
            sp_gu = 0
            for kc in range(KC):
                sp_gu = sc.pe(lambda e, kc=kc, hc=hc, gp=gp: e.matmul(
                    gu_ps[gp][:, 256:512],
                    wuT[:, kc * HID + hc * 128: kc * HID + (hc + 1) * 128],
                    xnT2[:, kc * 256:(kc + 1) * 256],
                    start=False, stop=(kc == KC - 1), skip_group_check=True),
                    group_end=(kc == KC - 1))
            way_silu[(tau, hc)] = sc.act(lambda e, gp=gp, hp=hp: e.activation(
                silu_sb[hp][:, :], gu_ps[gp][:, 0:256], AF.Silu, scale=S127),
                waits={"SP": sp_gu})
            way_hdn[(tau, hc)] = sc.dve(lambda e, gp=gp, hp=hp: e.tensor_mul(
                hdn[hp][:, :], gu_ps[gp][:, 256:512], silu_sb[hp][:, :]),
                waits={"SA": way_silu[(tau, hc)]})
            if hc >= 2:
                emit_down(tau, hc - 2)
        emit_down(tau, HC - 2)
        emit_down(tau, HC - 1)
        if tau == 0:
            emit_post_collective()

        if tau + 1 < NT2:
            prep(tau + 1)

        for s in range(2):
            g = 2 * tau + s
            w = {S2SEM[g % 4]: lx_x2[g]}
            if tau >= 1:
                w["SL"] = way_fin[tau - 1][s]
            sc.dve(lambda e, g=g, s=s: e.tensor_scalar(
                C2[s][:, :], x2[g % 4][:, :], rstd_all[:, g:g + 1],
                negmur_all[:, g:g + 1], ALU.mult, ALU.add), waits=w)
            w = {"SV": fbox["sv_fb"]}
            if tau >= 1:
                w[("LXOA", "LXOB")[s]] = lx_out[g - 2]
                w["SL"] = way_c8[tau - 1][s]
            sc.dve(lambda e, s=s: e.tensor_add(B2[s][:, :], C2[s][:, :],
                                               F_b[:, :]), waits=w)
            w = {}
            if tau >= 1:
                w["SA"] = way_yexp[tau - 1][s]
                w["SL"] = way_ynmr[tau - 1][s]
            sc.dve(lambda e, s=s: e.bn_stats(bnt2[s][:, 0:6], B2[s][:, 0:512]),
                   waits=w)
            sc.dve(lambda e, s=s: e.bn_stats(bnt2[s][:, 6:12], B2[s][:, 512:1024]))
            way_aggr2[s] = sc.dve(lambda e, s=s: e.bn_aggr(ymv[s][:, :],
                                                           bnt2[s][:, :]))
        for s in range(2):
            g = 2 * tau + s
            w = {"SP": way_down[tau][s]}
            if tau >= 1:
                w["SL"] = way_fin[tau - 1][s]
            way_combx[g] = sc.dve(lambda e, g=g, s=s: e.scalar_tensor_tensor(
                A2[s][:, :], ffn_ps[s][:, 0:1024], S127 * S127, x2[g % 4][:, :],
                ALU.mult, ALU.add), waits=w)

        for s in range(2):
            g = 2 * tau + s
            w = {"SV": way_aggr2[s]}
            sa_l = sc.act(lambda e, s=s: e.activation(
                ylnv[s][:, :], ymv[s][:, 1:2], AF.Ln, bias=eps_col[:, :]),
                waits=w)
            we = {}
            if tau >= 1:
                we["SL"] = way_c8[tau - 1][s]
            way_yexp[tau][s] = sc.act(lambda e, s=s: e.activation(
                yrstd[s][:, :], ylnv[s][:, :], AF.Exp, scale=-0.5), waits=we)
            way_ynmr[tau][s] = sc.pool(lambda e, s=s: e.tensor_scalar(
                ynmr[s][:, :], ymv[s][:, 0:1], yrstd[s][:, :], -1.0,
                ALU.mult, ALU.mult),
                waits={"SA": way_yexp[tau][s], "SV": fbox["sv_fb"]})
            way_c8[tau][s] = sc.pool(lambda e, s=s: e.tensor_scalar(
                C2[s][:, :], B2[s][:, :], yrstd[s][:, :], ynmr[s][:, :],
                ALU.mult, ALU.add), waits={"SV": way_aggr2[s]})
            sc.pool(lambda e, s=s: e.tensor_mul(B2[s][:, :], C2[s][:, :],
                                                gam_b[:, :]))
            sc.pool(lambda e, s=s: e.tensor_add(C2[s][:, :], B2[s][:, :],
                                                bet_b[:, :]))
            sl_fin = sc.pool(lambda e, s=s: e.tensor_add(
                B2[s][:, :], A2[s][:, :], C2[s][:, :]),
                waits={"SV": way_combx[g]})
            way_fin[tau][s] = sl_fin
            lx_out[g] = sc.dma("sync", lambda e, g=g, s=s: e.dma_start(
                out=out_h[g * 128:(g + 1) * 128, :], in_=B2[s][:, :]),
                ("LXOA", "LXOB")[s], waits={"SL": sl_fin})

    with nc.Block() as block:
        emit_plan(sc, block, sems)

    S2es.close()
    s1bes.close()
    P2es.close()
    sems_es.close()
    es.close()
    return nc


def kernel(L0_attn_h, L0_attn_Wl, L0_attn_Wr, L0_attn_att_l, L0_attn_att_r,
           L0_attn_Wf, L0_attn_ln_gamma, L0_attn_ln_beta, L0_ffn_w_gate,
           L0_ffn_w_up, L0_ffn_w_down):
    x = np.asarray(L0_attn_h, dtype=np.float32)
    B, N, _ = x.shape
    T = (B * N) // N_CORES
    nc = build_full(T)
    in_maps = make_in_maps(x, L0_attn_Wr, L0_attn_Wf, L0_attn_att_r,
                           L0_attn_ln_gamma, L0_attn_ln_beta, L0_ffn_w_gate,
                           L0_ffn_w_up, L0_ffn_w_down)
    res = run_bass_kernel_spmd(nc, in_maps, core_ids=list(range(N_CORES)))
    out = np.concatenate([np.asarray(r["out"]) for r in res.results], axis=0)
    return out.reshape(B, N, D).astype(np.float32)


def make_in_maps(x, Wr, Wf, att_r, gam, bet, wg, wu, wd):
    B, N, _ = x.shape
    T = (B * N) // N_CORES
    wrT = np.ascontiguousarray(np.asarray(Wr, np.float32).T)
    wfT = np.ascontiguousarray(np.asarray(Wf, np.float32).T)
    wgT = np.ascontiguousarray(np.asarray(wg, np.int32).T)
    wuT = np.ascontiguousarray(np.asarray(wu, np.int32).T)
    wdT = np.ascontiguousarray(np.asarray(wd, np.int32).T)
    attr = np.asarray(att_r, np.float32)
    gam = np.ascontiguousarray(np.asarray(gam, np.float32).reshape(1, D))
    bet = np.ascontiguousarray(np.asarray(bet, np.float32).reshape(1, D))
    idn = np.eye(128, dtype=np.float32).astype(ml_dtypes.bfloat16)
    ones_bf = np.ones((128, 1), np.float32).astype(ml_dtypes.bfloat16)
    ones_row = np.ones((1, 128), np.float32)
    ones_rowb = np.ones((1, 128), np.float32).astype(ml_dtypes.bfloat16)
    xf = np.asarray(x, np.float32).reshape(B * N, D)
    in_maps = []
    for k in range(N_CORES):
        in_maps.append({
            "x": np.ascontiguousarray(xf[k * T:(k + 1) * T, :]),
            "wrT": wrT, "wfT": wfT, "wgT": wgT, "wuT": wuT, "wdT": wdT,
            "attr_row": np.ascontiguousarray(attr.reshape(1, HD)),
            "attr_col": np.ascontiguousarray(attr.reshape(HD, 1)),
            "gam": gam, "bet": bet, "idn": idn, "ones_bf": ones_bf,
            "ones_row": ones_row, "ones_rowb": ones_rowb,
        })
    return in_maps


# version 12
# speedup vs baseline: 1.0346x; 1.0070x over previous
import numpy as np
import ml_dtypes
from contextlib import ExitStack

import concourse.bass as bass
import concourse.mybir as mybir
from concourse.bass_utils import run_bass_kernel_spmd

F32 = mybir.dt.float32
BF16 = mybir.dt.bfloat16
I32 = mybir.dt.int32
AF = mybir.ActivationFunctionType
ALU = mybir.AluOpType
AX = mybir.AxisListType

D = 1024
H = 8
HD = 128
HID = 3072
KC = D // 128
HC = HID // 128
S127 = 1.0 / 127.0
EPS = 1e-5
N_CORES = 8

SYNC_SEMS = ("LXM", "S1A", "S1B", "S1C", "S2A", "S2B", "S2C", "S2D",
             "LXOA", "LXOB")
GP_SEMS = ("LWA", "LWR", "LWG", "LWD", "LWF", "LWF2", "LWB", "LWC", "LWGB")
ALL_SEMS = SYNC_SEMS + GP_SEMS + ("CC", "SA", "SV", "SP", "SL")


class Sched:

    def __init__(self):
        self.plan = []
        self.cnt = {k: 0 for k in ALL_SEMS}

    def op(self, engine, fn, waits=None, inc=None):
        waits = {k: v for k, v in (waits or {}).items() if v and v > 0}
        incs_abs = []
        for sem, amount in (inc or []):
            self.cnt[sem] += amount
            incs_abs.append((sem, amount, self.cnt[sem]))
        self.plan.append((engine, fn, waits, incs_abs))
        return {s: a for s, _, a in incs_abs}

    def dma(self, engine, fn, sem, waits=None):
        return self.op(engine, fn, waits, inc=[(sem, 16)])[sem]

    def act(self, fn, waits=None):
        w = dict(waits or {})
        w["SA"] = max(w.get("SA", 0), self.cnt["SA"])
        return self.op("scalar", fn, w, inc=[("SA", 1)])["SA"]

    def dve(self, fn, waits=None):
        w = dict(waits or {})
        w["SV"] = max(w.get("SV", 0), self.cnt["SV"])
        return self.op("vector", fn, w, inc=[("SV", 1)])["SV"]

    def pool(self, fn, waits=None):
        w = dict(waits or {})
        w["SL"] = max(w.get("SL", 0), self.cnt["SL"])
        return self.op("gpsimd", fn, w, inc=[("SL", 1)])["SL"]

    def pe(self, fn, waits=None, group_end=True):
        if group_end:
            return self.op("tensor", fn, waits, inc=[("SP", 1)])["SP"]
        self.op("tensor", fn, waits)
        return None


def emit_plan(sc, block, sems):
    def emit(engine_name, eng):
        high = {}
        for (en, fn, waits, incs) in sc.plan:
            if en != engine_name:
                continue
            for s, v in sorted(waits.items()):
                if high.get(s, 0) >= v:
                    continue
                eng.wait_ge(sems[s], v)
                high[s] = v
            inst = fn(eng)
            for s, amount, absval in incs:
                inst = inst.then_inc(sems[s], amount)

    @block.sync
    def _(sync):
        emit("sync", sync)
        for s in SYNC_SEMS:
            if sc.cnt[s] > 0:
                sync.wait_ge(sems[s], sc.cnt[s])

    @block.gpsimd
    def _(g):
        emit("gpsimd", g)
        for s in GP_SEMS:
            if sc.cnt[s] > 0:
                g.wait_ge(sems[s], sc.cnt[s])

    @block.scalar
    def _(s):
        emit("scalar", s)

    @block.vector
    def _(v):
        emit("vector", v)

    @block.tensor
    def _(t):
        emit("tensor", t)


def build_full(T):
    NT1 = T // 128
    NT2 = T // 256
    nc = bass.Bass("TRN2", target_bir_lowering=False, debug=False)

    x_h = nc.dram_tensor("x", [T, D], F32, kind="ExternalInput").ap()
    wrT_h = nc.dram_tensor("wrT", [D, D], F32, kind="ExternalInput").ap()
    wfT_h = nc.dram_tensor("wfT", [D, D], F32, kind="ExternalInput").ap()
    wgT_h = nc.dram_tensor("wgT", [D, HID], I32, kind="ExternalInput").ap()
    wuT_h = nc.dram_tensor("wuT", [D, HID], I32, kind="ExternalInput").ap()
    wdT_h = nc.dram_tensor("wdT", [HID, D], I32, kind="ExternalInput").ap()
    attr_row_h = nc.dram_tensor("attr_row", [1, HD], F32, kind="ExternalInput").ap()
    attr_col_h = nc.dram_tensor("attr_col", [HD, 1], F32, kind="ExternalInput").ap()
    gam_h = nc.dram_tensor("gam", [1, D], F32, kind="ExternalInput").ap()
    bet_h = nc.dram_tensor("bet", [1, D], F32, kind="ExternalInput").ap()
    idn_h = nc.dram_tensor("idn", [128, 128], BF16, kind="ExternalInput").ap()
    ones_bf_h = nc.dram_tensor("ones_bf", [128, 1], BF16, kind="ExternalInput").ap()
    ones_row_h = nc.dram_tensor("ones_row", [1, 128], F32, kind="ExternalInput").ap()
    ones_rowb_h = nc.dram_tensor("ones_rowb", [1, 128], BF16, kind="ExternalInput").ap()
    out_h = nc.dram_tensor("out", [T, D], F32, kind="ExternalOutput").ap()
    bounce_in = nc.dram_tensor("bounce_in", [128, 16], F32)
    bounce_out = nc.dram_tensor("bounce_out", [128, 16], F32)

    es = ExitStack()
    SBp = lambda n, s, d: es.enter_context(nc.sbuf_tensor(n, s, d))

    wgT = SBp("wgT_sb", [128, KC * HID], BF16)
    wuT = SBp("wuT_sb", [128, KC * HID], BF16)
    wdT = SBp("wdT_sb", [128, HC * D], BF16)
    idn = SBp("idn_sb", [128, 128], BF16)
    ones_bf = SBp("ones_bf_sb", [128, 1], BF16)
    ones_row = SBp("ones_row_sb", [1, 128], F32)
    ones_rowb = SBp("ones_rowb_sb", [1, 128], BF16)
    attr_col = SBp("attr_col_sb", [HD, 1], F32)
    recip_a = SBp("recip_a", [HD, 1], F32)
    rstd_all = SBp("rstd_all", [128, NT1], F32)
    negmur_all = SBp("negmur_all", [128, NT1], F32)
    ct_bf = SBp("ct_bf", [128, H], BF16)
    eps_col = SBp("eps_col", [128, 1], F32)
    gam_b = SBp("gam_b", [128, D], BF16)
    bet_b = SBp("bet_b", [128, D], BF16)

    tr_ps = es.enter_context(nc.psum_tensor("tr_ps", [128, 1024], BF16))
    pes = ExitStack()
    fr_ps = [pes.enter_context(nc.psum_tensor(f"fr_ps{i}", [128, 1024], F32))
             for i in range(2)]
    ct_ps = pes.enter_context(nc.psum_tensor("ct_ps", [128, 512], F32))
    bc_ps = pes.enter_context(nc.psum_tensor("bc_ps", [128, 1024], F32))

    s1bes = ExitStack()
    S1B = lambda n, s, d: s1bes.enter_context(nc.sbuf_tensor(n, s, d))
    wfc = [S1B(f"wfc_{i}", [128, D], BF16) for i in range(2)]
    ctall = S1B("ctall", [128, 8], F32)
    sall = S1B("sall", [1, 8], F32)
    sall_r = S1B("sall_r", [1, 8], F32)
    sall_rb = S1B("sall_rb", [1, 8], BF16)
    frow_b = S1B("frow_b", [1, D], BF16)
    rs_b = S1B("rs_b", [128, 8], F32)
    ct1 = S1B("ct1", [128, 8], F32)
    ct2 = S1B("ct2", [128, 8], F32)

    s1es = ExitStack()
    S1 = lambda n, s, d: s1es.enter_context(nc.sbuf_tensor(n, s, d))
    wrT_s = S1("wrT_s", [128, KC * D], BF16)
    attr_rep = S1("attr_rep", [128, D], BF16)
    x1 = [S1(f"x1_{i}", [128, D], F32) for i in range(3)]
    xn1 = [S1(f"xn1_{i}", [128, D], BF16) for i in range(2)]
    xnT1 = [S1(f"xnT1_{i}", [128, D], BF16) for i in range(2)]
    frbf = [S1(f"frbf_{i}", [128, D], BF16) for i in range(2)]
    bnt1 = [S1(f"bnt1_{i}", [128, 12], F32) for i in range(2)]
    mv1 = [S1(f"mv1_{i}", [128, 2], F32) for i in range(2)]
    lnv1 = [S1(f"lnv1_{i}", [128, 1], F32) for i in range(2)]
    srab = [S1(f"srab_{i}", [128, 16], F32) for i in range(2)]
    srs = [S1(f"srs_{i}", [128, 8], F32) for i in range(2)]
    e_bf = [S1(f"e_bf_{i}", [128, 8], BF16) for i in range(2)]
    cs_stage = S1("cs_stage", [128, 16], F32)

    sems_es = ExitStack()
    sems = {k: sems_es.enter_context(nc.semaphore(k)) for k in ALL_SEMS}
    sc = Sched()

    sc.dma("sync", lambda e: e.dma_start(out=idn[:, :], in_=idn_h), "LXM")
    sc.dma("sync", lambda e: e.dma_start(out=ones_bf[:, :], in_=ones_bf_h), "LXM")
    sc.dma("sync", lambda e: e.dma_start(out=ones_row[:, :], in_=ones_row_h), "LXM")
    sc.dma("sync", lambda e: e.dma_start(out=ones_rowb[:, :], in_=ones_rowb_h), "LXM")
    sc.dma("sync", lambda e: e.dma_start(out=attr_col[:, :], in_=attr_col_h), "LXM")
    LXM_T = sc.cnt["LXM"]

    for c in range(KC):
        sc.dma("gpsimd", lambda e, c=c: e.dma_start(
            out=attr_rep[0:1, c * HD:(c + 1) * HD], in_=attr_row_h), "LWA")
    LWA_T = sc.cnt["LWA"]
    sc.dma("gpsimd", lambda e: e.dma_start(out=gam_b[0:1, :], in_=gam_h), "LWGB")
    sc.dma("gpsimd", lambda e: e.dma_start(out=bet_b[0:1, :], in_=bet_h), "LWGB")
    sc.dma("gpsimd", lambda e: e.dma_start(
        out=wrT_s[:, :].rearrange("p (kc n) -> p kc n", kc=KC),
        in_=wrT_h.rearrange("(kc p) n -> p kc n", p=128)), "LWR")
    sc.dma("gpsimd", lambda e: e.dma_start(
        out=wgT[:, :].rearrange("p (kc n) -> p kc n", kc=KC),
        in_=wgT_h.rearrange("(kc p) n -> p kc n", p=128)), "LWG")
    sc.dma("gpsimd", lambda e: e.dma_start(
        out=wuT[:, :].rearrange("p (kc n) -> p kc n", kc=KC),
        in_=wuT_h.rearrange("(kc p) n -> p kc n", p=128)), "LWG")
    sc.dma("gpsimd", lambda e: e.dma_start(
        out=wdT[:, :].rearrange("p (hc n) -> p hc n", hc=HC),
        in_=wdT_h.rearrange("(hc p) n -> p hc n", p=128)), "LWD")

    sv_eps = sc.dve(lambda e: e.memset(eps_col[:, :], EPS))
    sc.dve(lambda e: e.memset(cs_stage[:, :], 0.0))
    sv_recipa = sc.dve(lambda e: e.reciprocal(recip_a[:, :], attr_col[:, :]),
                       waits={"LXM": LXM_T})

    def bcast_bf(row_tile, w0):
        sp = 0
        for half in range(2):
            sp = sc.pe(lambda e, half=half: e.matmul(
                bc_ps[:, half * 512:(half + 1) * 512], ones_rowb[0:1, :],
                row_tile[0:1, half * 512:(half + 1) * 512], start=True,
                stop=True, skip_group_check=True),
                waits=w0 if half == 0 else None, group_end=(half == 1))
        return sc.dve(lambda e: e.tensor_copy(row_tile[:, :], bc_ps[:, 0:1024]),
                      waits={"SP": sp})

    sv_attr = bcast_bf(attr_rep, {"LWA": LWA_T, "LXM": LXM_T})
    sv_gb = bcast_bf(gam_b, {"LWGB": 32, "SV": sv_attr})
    sv_bb = bcast_bf(bet_b, {"LWGB": 32, "SV": sv_gb})

    sv_wrs = 0
    for c in range(KC):
        sc.dve(lambda e, c=c: e.tensor_mul(frbf[0][:, :],
                                           wrT_s[:, c * D:(c + 1) * D],
                                           attr_rep[:, :]),
               waits={"SV": sv_attr, "LWR": 16})
        sv_wrs = sc.dve(lambda e, c=c: e.tensor_copy(wrT_s[:, c * D:(c + 1) * D],
                                                     frbf[0][:, :]))


    way_xn = [0] * NT1
    way_stats = [0] * NT1
    way_evac = [0] * NT1
    way_red = [0] * NT1
    way_frbf = [0] * NT1
    way_trg = [0] * NT1
    way_frg = [0] * NT1
    way_csg = [0] * NT1
    way_e = [0] * NT1
    way_srs = [0] * NT1
    lx_x1 = [0] * NT1
    S1SEM = ("S1A", "S1B", "S1C")

    def s1_load(t):
        w = {}
        if t >= 3:
            w = {"SA": way_xn[t - 3], "SV": way_stats[t - 3]}
        lx_x1[t] = sc.dma("sync", lambda e, t=t: e.dma_start(
            out=x1[t % 3][:, :], in_=x_h[t * 128:(t + 1) * 128, :]),
            S1SEM[t % 3], waits=w)

    def s1_stats(t):
        p = t % 2
        w = {S1SEM[t % 3]: lx_x1[t]}
        if t >= 2:
            w["SA"] = way_xn[t - 2]
        sc.dve(lambda e, t=t, p=p: e.bn_stats(bnt1[p][:, 0:6],
                                              x1[t % 3][:, 0:512]), waits=w)
        sc.dve(lambda e, t=t, p=p: e.bn_stats(bnt1[p][:, 6:12],
                                              x1[t % 3][:, 512:1024]))
        way_stats[t] = sc.dve(lambda e, p=p: e.bn_aggr(mv1[p][:, :],
                                                       bnt1[p][:, :]))

    def s1_lnxn(t):
        p = t % 2
        sa_ln = sc.act(lambda e, p=p: e.activation(
            lnv1[p][:, :], mv1[p][:, 1:2], AF.Ln, bias=eps_col[:, :]),
            waits={"SV": max(way_stats[t], sv_eps)})
        sa_rstd = sc.act(lambda e, t=t, p=p: e.activation(
            rstd_all[:, t:t + 1], lnv1[p][:, :], AF.Exp, scale=-0.5))
        sl_nmr = sc.pool(lambda e, t=t, p=p: e.tensor_scalar(
            negmur_all[:, t:t + 1], mv1[p][:, 0:1], rstd_all[:, t:t + 1], -1.0,
            ALU.mult, ALU.mult), waits={"SA": sa_rstd})
        w = {"SL": sl_nmr, S1SEM[t % 3]: lx_x1[t]}
        if t >= 2:
            w["SP"] = way_trg[t - 2]
        way_xn[t] = sc.act(lambda e, p=p, t=t: e.activation(
            xn1[p][:, :], x1[t % 3][:, :], AF.Identity,
            bias=negmur_all[:, t:t + 1], scale=rstd_all[:, t:t + 1]), waits=w)

    def s1_cs(t):
        p = t % 2
        w = {"SA": max(way_e[t], way_frbf[t]), "LXM": LXM_T}
        for h in range(H):
            sc.pe(lambda e, h=h, p=p, t=t: e.matmul(
                ct_ps[:, h:h + 1], frbf[p][:, h * HD:(h + 1) * HD],
                e_bf[p][:, h:h + 1], start=(t == 0 and h == 0),
                stop=(t == NT1 - 1), skip_group_check=True),
                waits=w if h == 0 else None, group_end=False)
        way_csg[t] = sc.pe(lambda e, p=p, t=t: e.matmul(
            ct_ps[0:1, 8:16], ones_bf[:, :], e_bf[p][:, :],
            start=False, stop=(t == NT1 - 1), skip_group_check=True))

    def s1_tr(t):
        p = t % 2
        w = {"SA": way_xn[t], "LXM": LXM_T}
        if t >= 1:
            w["SV"] = way_evac[t - 1]
        for c in range(KC):
            way_trg[t] = sc.pe(lambda e, c=c, p=p: e.transpose(
                tr_ps[:, c * 128:(c + 1) * 128], xn1[p][:, c * 128:(c + 1) * 128],
                idn[:, :]), waits=w if c == 0 else None, group_end=(c == KC - 1))
        way_evac[t] = sc.dve(lambda e, t=t: e.tensor_copy(
            xnT1[t % 2][:, :], tr_ps[:, :]), waits={"SP": way_trg[t]})

    def s1_fr(t):
        p = t % 2
        w = {"SV": max(way_evac[t], sv_wrs)}
        if t >= 2:
            w["SA"] = max(w.get("SA", 0), way_frbf[t - 2])
        for half in range(2):
            for kc in range(KC):
                way_frg[t] = sc.pe(lambda e, half=half, kc=kc, p=p, t=t: e.matmul(
                    fr_ps[p][:, half * 512:(half + 1) * 512],
                    xnT1[t % 2][:, kc * 128:(kc + 1) * 128],
                    wrT_s[:, kc * D + half * 512: kc * D + (half + 1) * 512],
                    start=(kc == 0), stop=(kc == KC - 1), skip_group_check=True),
                    waits=w if (half == 0 and kc == 0) else None,
                    group_end=(half == 1 and kc == KC - 1))

    def s1_post(t):
        p = t % 2
        w = {"SP": way_frg[t]}
        if t >= 2:
            w["SP"] = max(w["SP"], way_csg[t - 2])
            w["SV"] = way_red[t - 2]
        way_frbf[t] = sc.act(lambda e, p=p: e.activation(
            frbf[p][:, :], fr_ps[p][:, :], AF.Copy), waits=w)
        sc.dve(lambda e, p=p: e.reduce_sum(
            srab[p][:, 0:8],
            frbf[p][:, :].rearrange("q (h j) -> q h j", h=H), axis=AX.X),
            waits={"SA": way_frbf[t]})
        way_red[t] = sc.dve(lambda e, p=p: e.reduce_sum(
            srab[p][:, 8:16],
            frbf[p][:, :].rearrange("q (h j) -> q h j", h=H), axis=AX.X,
            apply_absolute_value=True))
        way_srs[t] = sc.dve(lambda e, p=p: e.scalar_tensor_tensor(
            srs[p][:, :], srab[p][:, 0:8], 0.505 / 0.495, srab[p][:, 8:16],
            ALU.mult, ALU.add))
        w = {"SV": way_srs[t]}
        if t >= 2:
            w["SP"] = max(w.get("SP", 0), way_csg[t - 2])
        way_e[t] = sc.act(lambda e, p=p: e.activation(
            e_bf[p][:, :], srs[p][:, :], AF.Exp, scale=0.495), waits=w)

    for i in range(NT1 + 4):
        if i < NT1:
            s1_load(i)
        if 1 <= i <= NT1:
            s1_stats(i - 1)
            s1_lnxn(i - 1)
            s1_tr(i - 1)
        if 2 <= i <= NT1 + 1:
            s1_fr(i - 2)
        if 3 <= i <= NT1 + 2:
            s1_post(i - 3)
        if 4 <= i <= NT1 + 3:
            s1_cs(i - 4)

    sv_ct = sc.dve(lambda e: e.tensor_copy(cs_stage[:, 0:8], ct_ps[:, 0:8]),
                   waits={"SP": way_csg[NT1 - 1]})
    sv_s = sc.dve(lambda e: e.tensor_copy(cs_stage[0:1, 8:16], ct_ps[0:1, 8:16]))
    sc.dma("gpsimd", lambda e: e.dma_start(
        out=bounce_in.ap()[:, :], in_=cs_stage[:, :]), "LWB", waits={"SV": sv_s})
    sc.op("gpsimd", lambda e: e.collective_compute(
        "AllReduce", ALU.add, replica_groups=[[0, 1, 2, 3], [4, 5, 6, 7]],
        ins=[bounce_in.ap().opt()], outs=[bounce_out.ap().opt()]),
        waits={"LWB": 16}, inc=[("CC", 1)])
    sc.dma("gpsimd", lambda e: e.dma_start(
        out=ctall[:, :], in_=bounce_out.ap()[:, 0:8]), "LWC", waits={"CC": 1})
    sc.dma("gpsimd", lambda e: e.dma_start(
        out=sall[:, :], in_=bounce_out.ap()[0:1, 8:16]), "LWC")


    s1es.close()
    S2es = ExitStack()
    S2 = lambda n, s, d: S2es.enter_context(nc.sbuf_tensor(n, s, d))
    F_b = S2("F_b", [128, D], F32)
    x2 = [S2(f"x2_{i}", [128, D], F32) for i in range(4)]
    xn2 = S2("xn2", [128, D], BF16)
    xnT2 = S2("xnT2", [128, KC * 256], BF16)
    hdn = [S2(f"hdn_{i}", [128, 256], BF16) for i in range(3)]
    silu_sb = [S2(f"silu_{i}", [128, 256], BF16) for i in range(3)]
    A2 = [S2(f"A2_{i}", [128, D], F32) for i in range(2)]
    B2 = [S2(f"B2_{i}", [128, D], F32) for i in range(2)]
    C2 = [S2(f"C2_{i}", [128, D], BF16) for i in range(2)]
    bnt2 = [S2(f"bnt2_{i}", [128, 12], F32) for i in range(2)]
    ymv = [S2(f"ymv_{i}", [128, 2], F32) for i in range(2)]
    ylnv = [S2(f"ylnv_{i}", [128, 1], F32) for i in range(2)]
    yrstd = [S2(f"yrstd_{i}", [128, 1], F32) for i in range(2)]
    ynmr = [S2(f"ynmr_{i}", [128, 1], F32) for i in range(2)]

    pes.close()
    P2es = ExitStack()
    gu_ps = [P2es.enter_context(nc.psum_tensor(f"gu_ps{i}", [128, 512], F32))
             for i in range(3)]
    ffn_ps = [P2es.enter_context(nc.psum_tensor(f"ffn_ps{i}", [128, 1024], F32))
              for i in range(2)]


    way_xnT2s = [0] * (2 * NT2)
    way_combx = [0] * (2 * NT2)
    way_hdn = {}
    way_silu = {}
    way_down = [[0, 0] for _ in range(NT2)]
    way_yexp = [[0, 0] for _ in range(NT2)]
    way_c8 = [[0, 0] for _ in range(NT2)]
    way_fin = [[0, 0] for _ in range(NT2)]
    way_ynmr = [[0, 0] for _ in range(NT2)]
    way_aggr2 = [0, 0]
    lx_x2 = [0] * (2 * NT2)
    lx_out = [0] * (2 * NT2)
    sa_xn2 = [0] * (2 * NT2)
    sp_tr2 = [0] * (2 * NT2)
    S2SEM = ("S2A", "S2B", "S2C", "S2D")

    S1END = way_csg[NT1 - 1]

    def prep(tau):
        for s in range(2):
            g = 2 * tau + s
            w = {"SP": S1END, "LWB": 16}
            if g >= 4:
                w["SV"] = way_combx[g - 4]
            lx_x2[g] = sc.dma("sync", lambda e, g=g: e.dma_start(
                out=x2[g % 4][:, :], in_=x_h[g * 128:(g + 1) * 128, :]),
                S2SEM[g % 4], waits=w)
            w = {S2SEM[g % 4]: lx_x2[g]}
            if g >= 1:
                w["SP"] = max(S1END, sp_tr2[g - 1])
            sa_xn2[g] = sc.act(lambda e, g=g: e.activation(
                xn2[:, :], x2[g % 4][:, :], AF.Identity,
                bias=negmur_all[:, g:g + 1], scale=rstd_all[:, g:g + 1]), waits=w)
            w = {"SA": sa_xn2[g], "LXM": LXM_T}
            if g >= 1:
                w["SV"] = way_xnT2s[g - 1]
            for kc in range(KC):
                sp_tr2[g] = sc.pe(lambda e, kc=kc: e.transpose(
                    tr_ps[:, kc * 128:(kc + 1) * 128],
                    xn2[:, kc * 128:(kc + 1) * 128], idn[:, :]),
                    waits=w if kc == 0 else None, group_end=(kc == KC - 1))
            way_xnT2s[g] = sc.dve(lambda e, s=s: e.tensor_copy(
                xnT2[:, :].rearrange("q (kc n) -> q kc n", kc=KC)
                [:, :, s * 128:(s + 1) * 128],
                tr_ps[:, :].rearrange("q (kc n) -> q kc n", kc=KC)),
                waits={"SP": sp_tr2[g]})

    def emit_down(tau, hc):
        hp = hc % 3
        w = {"SV": way_hdn[(tau, hc)], "LWD": 16}
        if hc == 0 and tau >= 1:
            w["SV"] = max(w["SV"], way_combx[2 * tau - 1])
        for s in range(2):
            for half in range(2):
                last = hc == HC - 1 and half == 1
                r = sc.pe(lambda e, s=s, half=half, hc=hc, hp=hp: e.matmul(
                    ffn_ps[s][:, half * 512:(half + 1) * 512],
                    hdn[hp][:, s * 128:(s + 1) * 128],
                    wdT[:, hc * D + half * 512: hc * D + (half + 1) * 512],
                    start=(hc == 0), stop=(hc == HC - 1), skip_group_check=True),
                    waits=w if (s == 0 and half == 0) else None, group_end=last)
                if last:
                    way_down[tau][s] = r

    fbox = {}

    def emit_post_collective():
        sc.dve(lambda e: e.reciprocal(sall_r[:, :], sall[:, :]),
               waits={"LWC": 32})
        sv_sr = sc.dve(lambda e: e.tensor_copy(sall_rb[:, :], sall_r[:, :]))
        wpe = {"SV": max(sv_sr, way_hdn[(0, HC - 1)]),
               "SA": way_silu[(0, HC - 1)]}
        sp_rs = sc.pe(lambda e: e.matmul(gu_ps[2][:, 0:8], ones_rowb[0:1, :],
                                         sall_rb[0:1, :], start=True, stop=True,
                                         skip_group_check=True), waits=wpe)
        sv_rsb = sc.dve(lambda e: e.tensor_copy(rs_b[:, :], gu_ps[2][:, 0:8]),
                        waits={"SP": sp_rs})
        sc.dve(lambda e: e.tensor_mul(ct1[:, :], ctall[:, :], rs_b[:, :]),
               waits={"SV": sv_rsb, "LWC": 32})
        sv_ct2 = sc.dve(lambda e: e.tensor_scalar(
            ct2[:, :], ct1[:, :], recip_a[:, :], None, ALU.mult),
            waits={"SV": sv_recipa})
        sa_ctbf = sc.act(lambda e: e.activation(ct_bf[:, :], ct2[:, :],
                                                AF.Copy), waits={"SV": sv_ct2})
        sp_fh = [0] * H

        def wf_load(h, w=None):
            return sc.dma("gpsimd", lambda e, h=h: e.dma_start(
                out=wfc[h % 2][:, :], in_=wfT_h[h * 128:(h + 1) * 128, :]),
                ("LWF", "LWF2")[h % 2], waits=w)

        wf_load(0)
        wf_load(1)
        for h in range(H):
            w = {("LWF", "LWF2")[h % 2]: 16 * (h // 2 + 1), "SA": sa_ctbf}
            if h == 0:
                w["SV"] = max(sv_ct2, way_hdn[(0, HC - 1)])
                w["SA"] = max(sa_ctbf, way_silu[(0, HC - 1)])
            for half in range(2):
                sp_fh[h] = sc.pe(lambda e, half=half, h=h: e.matmul(
                    gu_ps[half][0:1, 0:512], ct_bf[:, h:h + 1],
                    wfc[h % 2][:, half * 512:(half + 1) * 512],
                    start=(h == 0), stop=(h == H - 1), skip_group_check=True),
                    waits=w if half == 0 else None, group_end=(half == 1))
            if h + 2 < H:
                wf_load(h + 2, w={"SP": sp_fh[h]})
        sv_fev = 0
        for half in range(2):
            sv_fev = sc.dve(lambda e, half=half: e.tensor_copy(
                frow_b[0:1, half * 512:(half + 1) * 512],
                gu_ps[half][0:1, 0:512]), waits={"SP": sp_fh[H - 1]})
        sp_fb = [0, 0]
        for half in range(2):
            sp_fb[half] = sc.pe(lambda e, half=half: e.matmul(
                gu_ps[half][:, 0:512], ones_rowb[0:1, :],
                frow_b[0:1, half * 512:(half + 1) * 512], start=True, stop=True,
                skip_group_check=True),
                waits={"SV": sv_fev} if half == 0 else None, group_end=True)
        sv_fb = 0
        for half in range(2):
            sv_fb = sc.dve(lambda e, half=half: e.tensor_copy(
                F_b[:, half * 512:(half + 1) * 512], gu_ps[half][:, 0:512]),
                waits={"SP": sp_fb[half]})
        fbox["sv_fb"] = sv_fb

    prep(0)
    for tau in range(NT2):
        for hc in range(HC):
            hp = hc % 3
            gp = hc % 3
            w = {"SV": way_xnT2s[2 * tau + 1], "LWG": 32}
            if tau == 1 and hc == 0:
                w["SV"] = max(w["SV"], fbox["sv_fb"])
            if hc >= 3:
                w["SV"] = max(w["SV"], way_hdn[(tau, hc - 3)])
                w["SA"] = way_silu[(tau, hc - 3)]
            elif tau >= 1:
                w["SV"] = max(w["SV"], way_hdn[(tau - 1, HC - 3 + hc)])
                w["SA"] = way_silu[(tau - 1, HC - 3 + hc)]
            for kc in range(KC):
                sc.pe(lambda e, kc=kc, hc=hc, gp=gp: e.matmul(
                    gu_ps[gp][:, 0:256],
                    wgT[:, kc * HID + hc * 128: kc * HID + (hc + 1) * 128],
                    xnT2[:, kc * 256:(kc + 1) * 256],
                    start=(kc == 0), stop=(kc == KC - 1), skip_group_check=True),
                    waits=w if kc == 0 else None, group_end=False)
            sp_gu = 0
            for kc in range(KC):
                sp_gu = sc.pe(lambda e, kc=kc, hc=hc, gp=gp: e.matmul(
                    gu_ps[gp][:, 256:512],
                    wuT[:, kc * HID + hc * 128: kc * HID + (hc + 1) * 128],
                    xnT2[:, kc * 256:(kc + 1) * 256],
                    start=False, stop=(kc == KC - 1), skip_group_check=True),
                    group_end=(kc == KC - 1))
            way_silu[(tau, hc)] = sc.act(lambda e, gp=gp, hp=hp: e.activation(
                silu_sb[hp][:, :], gu_ps[gp][:, 0:256], AF.Silu, scale=S127),
                waits={"SP": sp_gu})
            way_hdn[(tau, hc)] = sc.dve(lambda e, gp=gp, hp=hp: e.tensor_mul(
                hdn[hp][:, :], gu_ps[gp][:, 256:512], silu_sb[hp][:, :]),
                waits={"SA": way_silu[(tau, hc)]})
            if hc >= 2:
                emit_down(tau, hc - 2)
        emit_down(tau, HC - 2)
        emit_down(tau, HC - 1)
        if tau == 0:
            emit_post_collective()

        if tau + 1 < NT2:
            prep(tau + 1)

        for s in range(2):
            g = 2 * tau + s
            w = {S2SEM[g % 4]: lx_x2[g]}
            if tau >= 1:
                w["SL"] = way_fin[tau - 1][s]
            sc.dve(lambda e, g=g, s=s: e.tensor_scalar(
                C2[s][:, :], x2[g % 4][:, :], rstd_all[:, g:g + 1],
                negmur_all[:, g:g + 1], ALU.mult, ALU.add), waits=w)
            w = {"SV": fbox["sv_fb"]}
            if tau >= 1:
                w[("LXOA", "LXOB")[s]] = lx_out[g - 2]
                w["SL"] = way_c8[tau - 1][s]
            sc.dve(lambda e, s=s: e.tensor_add(B2[s][:, :], C2[s][:, :],
                                               F_b[:, :]), waits=w)
            w = {}
            if tau >= 1:
                w["SA"] = way_yexp[tau - 1][s]
                w["SL"] = way_ynmr[tau - 1][s]
            sc.dve(lambda e, s=s: e.bn_stats(bnt2[s][:, 0:6], B2[s][:, 0:512]),
                   waits=w)
            sc.dve(lambda e, s=s: e.bn_stats(bnt2[s][:, 6:12], B2[s][:, 512:1024]))
            way_aggr2[s] = sc.dve(lambda e, s=s: e.bn_aggr(ymv[s][:, :],
                                                           bnt2[s][:, :]))
        for s in range(2):
            g = 2 * tau + s
            w = {"SP": way_down[tau][s]}
            if tau >= 1:
                w["SL"] = way_fin[tau - 1][s]
            way_combx[g] = sc.dve(lambda e, g=g, s=s: e.scalar_tensor_tensor(
                A2[s][:, :], ffn_ps[s][:, 0:1024], S127 * S127, x2[g % 4][:, :],
                ALU.mult, ALU.add), waits=w)

        for s in range(2):
            g = 2 * tau + s
            w = {"SV": way_aggr2[s]}
            sa_l = sc.act(lambda e, s=s: e.activation(
                ylnv[s][:, :], ymv[s][:, 1:2], AF.Ln, bias=eps_col[:, :]),
                waits=w)
            we = {}
            if tau >= 1:
                we["SL"] = way_c8[tau - 1][s]
            way_yexp[tau][s] = sc.act(lambda e, s=s: e.activation(
                yrstd[s][:, :], ylnv[s][:, :], AF.Exp, scale=-0.5), waits=we)
            way_ynmr[tau][s] = sc.pool(lambda e, s=s: e.tensor_scalar(
                ynmr[s][:, :], ymv[s][:, 0:1], yrstd[s][:, :], -1.0,
                ALU.mult, ALU.mult),
                waits={"SA": way_yexp[tau][s], "SV": fbox["sv_fb"]})
            way_c8[tau][s] = sc.pool(lambda e, s=s: e.tensor_scalar(
                C2[s][:, :], B2[s][:, :], yrstd[s][:, :], ynmr[s][:, :],
                ALU.mult, ALU.add), waits={"SV": way_aggr2[s]})
            sc.pool(lambda e, s=s: e.tensor_mul(B2[s][:, :], C2[s][:, :],
                                                gam_b[:, :]))
            sc.pool(lambda e, s=s: e.tensor_add(C2[s][:, :], B2[s][:, :],
                                                bet_b[:, :]))
            sl_fin = sc.pool(lambda e, s=s: e.tensor_add(
                B2[s][:, :], A2[s][:, :], C2[s][:, :]),
                waits={"SV": way_combx[g]})
            way_fin[tau][s] = sl_fin
            lx_out[g] = sc.dma("sync", lambda e, g=g, s=s: e.dma_start(
                out=out_h[g * 128:(g + 1) * 128, :], in_=B2[s][:, :]),
                ("LXOA", "LXOB")[s], waits={"SL": sl_fin})

    with nc.Block() as block:
        emit_plan(sc, block, sems)

    S2es.close()
    s1bes.close()
    P2es.close()
    sems_es.close()
    es.close()
    return nc


def kernel(L0_attn_h, L0_attn_Wl, L0_attn_Wr, L0_attn_att_l, L0_attn_att_r,
           L0_attn_Wf, L0_attn_ln_gamma, L0_attn_ln_beta, L0_ffn_w_gate,
           L0_ffn_w_up, L0_ffn_w_down):
    x = np.asarray(L0_attn_h, dtype=np.float32)
    B, N, _ = x.shape
    T = (B * N) // N_CORES
    nc = build_full(T)
    in_maps = make_in_maps(x, L0_attn_Wr, L0_attn_Wf, L0_attn_att_r,
                           L0_attn_ln_gamma, L0_attn_ln_beta, L0_ffn_w_gate,
                           L0_ffn_w_up, L0_ffn_w_down)
    res = run_bass_kernel_spmd(nc, in_maps, core_ids=list(range(N_CORES)))
    out = np.concatenate([np.asarray(r["out"]) for r in res.results], axis=0)
    return out.reshape(B, N, D).astype(np.float32)


def make_in_maps(x, Wr, Wf, att_r, gam, bet, wg, wu, wd):
    B, N, _ = x.shape
    T = (B * N) // N_CORES
    wrT = np.ascontiguousarray(np.asarray(Wr, np.float32).T)
    wfT = np.ascontiguousarray(np.asarray(Wf, np.float32).T)
    wgT = np.ascontiguousarray(np.asarray(wg, np.int32).T)
    wuT = np.ascontiguousarray(np.asarray(wu, np.int32).T)
    wdT = np.ascontiguousarray(np.asarray(wd, np.int32).T)
    attr = np.asarray(att_r, np.float32)
    gam = np.ascontiguousarray(np.asarray(gam, np.float32).reshape(1, D))
    bet = np.ascontiguousarray(np.asarray(bet, np.float32).reshape(1, D))
    idn = np.eye(128, dtype=np.float32).astype(ml_dtypes.bfloat16)
    ones_bf = np.ones((128, 1), np.float32).astype(ml_dtypes.bfloat16)
    ones_row = np.ones((1, 128), np.float32)
    ones_rowb = np.ones((1, 128), np.float32).astype(ml_dtypes.bfloat16)
    xf = np.asarray(x, np.float32).reshape(B * N, D)
    in_maps = []
    for k in range(N_CORES):
        in_maps.append({
            "x": np.ascontiguousarray(xf[k * T:(k + 1) * T, :]),
            "wrT": wrT, "wfT": wfT, "wgT": wgT, "wuT": wuT, "wdT": wdT,
            "attr_row": np.ascontiguousarray(attr.reshape(1, HD)),
            "attr_col": np.ascontiguousarray(attr.reshape(HD, 1)),
            "gam": gam, "bet": bet, "idn": idn, "ones_bf": ones_bf,
            "ones_row": ones_row, "ones_rowb": ones_rowb,
        })
    return in_maps
